# version 05
# baseline (speedup 1.0000x reference)
import numpy as np
import ml_dtypes
from contextlib import ExitStack

import concourse.bass as bass
import concourse.mybir as mybir
from concourse.bass_utils import run_bass_kernel_spmd

F32 = mybir.dt.float32
BF16 = mybir.dt.bfloat16
I32 = mybir.dt.int32
AF = mybir.ActivationFunctionType
ALU = mybir.AluOpType
AX = mybir.AxisListType

D = 2048
DC = 16
NH = 16
BF = ml_dtypes.bfloat16

CC_EPS_LN = 0
CC_3p75 = 1
CC_m8 = 2
CC_m10 = 3
CC_m4 = 4
CC_m6 = 5
CC_m2 = 6
NCC = 7

ENGS = ("sync", "tensor", "vector", "scalar", "gpsimd")


class Sem:
    def __init__(self, kb, name):
        self.h = kb.ctx.enter_context(kb.nc.semaphore(name))
        self.n = 0


class Ph:

    def __init__(self, kb, name):
        self.kb = kb
        self.name = name
        self.q = {e: [] for e in ENGS}

    def wait(self, eng, sem, val):
        if val > 0:
            self.q[eng].append(lambda e, s=sem.h, v=val: e.wait_ge(s, v))

    def emit(self, eng, fn, sem=None, k=1):
        if sem is None:
            self.q[eng].append(fn)
            return None
        sem.n += k
        self.q[eng].append(lambda e, f=fn, s=sem.h, kk=k: f(e).then_inc(s, kk))
        return sem.n

    def dma(self, eng, out, in_, sem, transpose=False):
        return self.emit(eng, lambda e, o=out, i=in_, t=transpose: e.dma_start(
            out=o, in_=i, transpose=t), sem=sem, k=16)

    def run(self):
        with self.kb.nc.Block(self.name) as blk:
            for eng in ENGS:
                qq = self.q[eng]
                if not qq and eng != "gpsimd":
                    continue

                def body(e, qq=qq):
                    for f in qq:
                        f(e)
                getattr(blk, eng)(body)


class KB:
    def __init__(self, nc, T, dump=()):
        self.nc = nc
        self.T = T
        self.TT = T // 512
        self.TC = T // 128
        self.dump = set(dump)
        self.ctx = ExitStack()

    def dram(self, name, shape, dtype, kind="Internal"):
        if name in self.dump:
            kind = "ExternalOutput"
        return self.nc.dram_tensor(name, shape, dtype, kind=kind).ap()

    def declare(self):
        E, T = self.ctx.enter_context, self.T
        self.x_in = self.dram("x", [T, D], F32, kind="ExternalInput")
        self.Wa = self.dram("W_attn", [D, 3 * D], F32, kind="ExternalInput")
        self.Wp = self.dram("W_proj", [D, D], F32, kind="ExternalInput")
        self.lamvec = self.dram("lamvec", [1, 4 * D], F32, kind="ExternalInput")
        self.g_pk = self.dram("g_pk", [8 * D, D // 2], I32, kind="ExternalInput")
        self.g_sc = self.dram("g_sc", [8 * D, D // 32], F32, kind="ExternalInput")
        self.u_pk = self.dram("u_pk", [8 * D, D // 2], I32, kind="ExternalInput")
        self.u_sc = self.dram("u_sc", [8 * D, D // 32], F32, kind="ExternalInput")
        self.d_pk = self.dram("d_pk", [D, 4 * D], I32, kind="ExternalInput")
        self.d_sc = self.dram("d_sc", [D, D // 4], F32, kind="ExternalInput")
        self.wq_in = self.dram("wq", [D, D], F32, kind="ExternalInput")
        self.wk_in = self.dram("wk", [D, D], F32, kind="ExternalInput")
        self.wv_in = self.dram("wv", [D, D], F32, kind="ExternalInput")
        self.wo_in = self.dram("wo", [D, D], F32, kind="ExternalInput")
        self.w1_in = self.dram("w1", [5632, D], F32, kind="ExternalInput")
        self.w2_in = self.dram("w2", [5632, D], F32, kind="ExternalInput")
        self.w3_in = self.dram("w3", [D, 5632], F32, kind="ExternalInput")
        self.c_idb = self.dram("idb", [128, 128], BF16, kind="ExternalInput")
        self.c_idf = self.dram("idf", [128, 128], F32, kind="ExternalInput")
        self.c_ones = self.dram("onesc", [128, 2], F32, kind="ExternalInput")
        self.c_onesb = self.dram("onesb", [128, 2], BF16, kind="ExternalInput")
        self.c_onesr = self.dram("onesr", [1, 128], F32, kind="ExternalInput")
        self.c_cc = self.dram("cc", [128, NCC], F32, kind="ExternalInput")
        self.c_rope = self.dram("rope", [64, 2 * T], F32, kind="ExternalInput")
        self.c_mask = self.dram("cmask", [4, 128, 512], BF16, kind="ExternalInput")
        self.y_out = self.dram("y", [T, D], F32, kind="ExternalOutput")
        self.xf_s = self.dram("xf_s", [self.TT, DC, 128, 512], F32)
        self.xn_s = self.dram("xn_s", [self.TT, DC, 128, 512], BF16)
        self.qs_s = self.dram("qs_s", [NH, 128, T], BF16)
        self.ks_s = self.dram("ks_s", [NH, 128, T], BF16)
        self.vs_s = self.dram("vs_s", [NH, self.TC, 128, 128], BF16)
        self.ao_s = self.dram("ao_s", [NH, 128, T], BF16)
        self.WgT_s = self.dram("WgT_s", [128, 128, D], BF16)
        self.WuT_s = self.dram("WuT_s", [128, 128, D], BF16)
        self.WdT2 = self.dram("WdT2", [4, 128, 128, 512], BF16)
        self.int_s = self.dram("int_s", [128, 128, T], BF16)
        self.wqT_s = self.dram("wqT_s", [16, 128, D], BF16)
        self.wkT_s = self.dram("wkT_s", [16, 128, D], BF16)
        self.wvT_s = self.dram("wvT_s", [16, 128, D], BF16)
        self.woT_s = self.dram("woT_s", [16, 128, D], BF16)
        self.w1T_s = self.dram("w1T_s", [44, 128, D], BF16)
        self.w2T_s = self.dram("w2T_s", [44, 128, D], BF16)
        self.w3T_s = self.dram("w3T_s", [16, 128, 5632], BF16)
        if "dbg1" in self.dump:
            self.dbg1 = self.dram("dbg1", [self.TC, 128, 24], F32)
        if "dbg5" in self.dump:
            self.dbg5 = self.dram("dbg5", [8, 128, 1024], F32, kind="ExternalOutput")
            self.dbg5b = self.dram("dbg5b", [128, 2048], BF16, kind="ExternalOutput")

        nc = self.nc
        self.ar = E(nc.sbuf_tensor("s_ar", [128, 65536], BF16))
        self.ar32 = self.ar.bitcast(F32)
        self.fb = E(nc.sbuf_tensor("s_fb", [128, 6144], F32))
        self.wb = E(nc.sbuf_tensor("s_wb", [128, 12288], BF16))
        self.ib = E(nc.sbuf_tensor("s_ib", [128, 2048], I32))
        self.smf = E(nc.sbuf_tensor("s_smf", [128, 96], F32))
        self.row = E(nc.sbuf_tensor("s_row", [1, 2048], F32))
        self.idb = E(nc.sbuf_tensor("s_idb", [128, 128], BF16))
        self.idf = E(nc.sbuf_tensor("s_idf", [128, 128], F32))
        self.onesc = E(nc.sbuf_tensor("s_onesc", [128, 2], F32))
        self.onesb = E(nc.sbuf_tensor("s_onesb", [128, 2], BF16))
        self.onesr = E(nc.sbuf_tensor("s_onesr", [1, 128], F32))
        self.cc = E(nc.sbuf_tensor("s_cc", [128, NCC], F32))
        self.q2s = E(nc.sbuf_tensor("s_q2s", [128, 1], F32))
        self.nub = E(nc.sbuf_tensor("s_nub", [128, 2], F32))
        self.thr = E(nc.sbuf_tensor("s_thr", [128, 4], F32))
        self.ps = [E(nc.psum_tensor(f"ps{i}", [128, 512], F32)) for i in range(8)]
        self.psb = [p.bitcast(BF16) for p in self.ps]

        self.msk = self.wb[:, 10240:12288]
        self.sL = Sem(self, "sem_L")
        self.sG = Sem(self, "sem_G")
        self.sM = Sem(self, "sem_M")
        self.sV = Sem(self, "sem_V")
        self.sA = Sem(self, "sem_A")
        self.sS = Sem(self, "sem_S")
        self.sP = Sem(self, "sem_P")
        self.sD = {}

    def sd(self, tag):
        if tag not in self.sD:
            self.sD[tag] = Sem(self, f"sd_{tag}")
        return self.sD[tag]

    def p0(self):
        kb = self
        ph = Ph(kb, "p0")
        for dst, src in [(kb.idb[:], kb.c_idb[:, :]), (kb.idf[:], kb.c_idf[:, :]),
                         (kb.onesc[:], kb.c_ones[:, :]), (kb.onesb[:], kb.c_onesb[:, :]),
                         (kb.onesr[:], kb.c_onesr[:, :]), (kb.cc[:], kb.c_cc[:, :]),
                         (kb.ar32[0:1, 0:4 * D], kb.lamvec[:, :])]:
            ph.dma("sync", dst, src, kb.sL)
        ld = kb.sL.n
        r = kb.ar32
        ph.wait("vector", kb.sL, ld)
        t0 = ph.emit("vector", lambda e: e.tensor_tensor(out=r[0:1, 4 * D:5 * D], in0=r[0:1, 0:D], in1=r[0:1, D:2 * D], op=ALU.mult), sem=kb.sV)
        ph.wait("vector", kb.sV, t0)
        t1 = ph.emit("vector", lambda e: e.reduce_sum(out=kb.smf[0:1, 64:65], in_=r[0:1, 4 * D:5 * D], axis=AX.X), sem=kb.sV)
        ph.wait("vector", kb.sV, t1)
        t2 = ph.emit("vector", lambda e: e.tensor_tensor(out=r[0:1, 4 * D:5 * D], in0=r[0:1, 2 * D:3 * D], in1=r[0:1, 3 * D:4 * D], op=ALU.mult), sem=kb.sV)
        ph.wait("vector", kb.sV, t2)
        v0 = ph.emit("vector", lambda e: e.reduce_sum(out=kb.smf[0:1, 65:66], in_=r[0:1, 4 * D:5 * D], axis=AX.X), sem=kb.sV)
        ph.wait("scalar", kb.sV, v0)
        a00 = ph.emit("scalar", lambda e: e.activation(out=kb.smf[0:1, 66:67], in_=kb.smf[0:1, 64:65], func=AF.Exp), sem=kb.sA)
        ph.wait("scalar", kb.sA, a00)
        a0 = ph.emit("scalar", lambda e: e.activation(out=kb.smf[0:1, 67:68], in_=kb.smf[0:1, 65:66], func=AF.Exp), sem=kb.sA)
        ph.wait("vector", kb.sA, a0)
        v05 = ph.emit("vector", lambda e: e.tensor_tensor(out=kb.smf[0:1, 68:69], in0=kb.smf[0:1, 66:67], in1=kb.smf[0:1, 67:68], op=ALU.subtract), sem=kb.sV)
        ph.wait("vector", kb.sV, v05)
        v1 = ph.emit("vector", lambda e: e.tensor_scalar(out=kb.smf[0:1, 68:69], in0=kb.smf[0:1, 68:69], scalar1=0.2, scalar2=-0.125, op0=ALU.add, op1=ALU.mult), sem=kb.sV)
        ph.wait("tensor", kb.sV, v1)
        ph.wait("tensor", kb.sL, ld)
        m0 = ph.emit("tensor", lambda e: e.matmul(kb.ps[0][:, 0:1], kb.onesr[:, :], kb.smf[0:1, 68:69], start=True, stop=True), sem=kb.sM)
        ph.wait("vector", kb.sM, m0)
        ph.emit("vector", lambda e: e.tensor_copy(out=kb.q2s[:], in_=kb.ps[0][:, 0:1]), sem=kb.sV)
        ph.run()

    def p1(self):
        kb = self
        ph = Ph(kb, "p1")
        TC = kb.TC
        ar32, fb, ar = kb.ar32, kb.fb, kb.ar
        XNST = 16384
        XNTK = 28672

        ld = [0] * TC
        a_sq = [0] * TC
        v_nb = [0] * TC
        a_xn = [0] * TC
        m_xf = [0] * TC
        m_xn = [0] * TC
        v_cp = {}
        st_tt = [0] * kb.TT
        tp_i = 0
        cp_order = []

        def xtok(sl):
            return fb[:, sl * 2048:(sl + 1) * 2048]

        def xntk(sl):
            return ar[:, XNTK + sl * 2048: XNTK + (sl + 1) * 2048]

        for tc in range(TC):
            sl = tc % 2
            tt = tc // 4
            if tc >= 2:
                ph.wait("sync", kb.sM, m_xf[tc - 2])
                ph.wait("sync", kb.sV, v_nb[tc - 2])
                ph.wait("sync", kb.sA, a_xn[tc - 2])
            sx = kb.sd(f"x{sl}")
            ld[tc] = ph.dma("sync", xtok(sl), kb.x_in[tc * 128:(tc + 1) * 128, :], sx)
            ph.wait("scalar", sx, ld[tc])
            if tc >= 2:
                ph.wait("scalar", kb.sM, m_xn[tc - 2])
                ph.wait("scalar", kb.sV, v_nb[tc - 2])
            c = sl
            ph.emit("scalar", lambda e, sl=sl, c=c: e.activation(
                out=xntk(sl), in_=xtok(sl), func=AF.Square,
                accum_out=kb.smf[:, 0 + c:1 + c]))
            a_sq[tc] = ph.emit("scalar", lambda e, c=c: e.copy(
                out=kb.smf[:, 0 + c:1 + c], in_=kb.smf[:, 0 + c:1 + c]), sem=kb.sA)
            ph.wait("vector", kb.sd(f"x{sl}"), ld[tc])
            ph.wait("vector", kb.sA, a_sq[tc])
            v_rs = ph.emit("vector", lambda e, sl=sl, c=c: e.reduce_sum(
                out=kb.smf[:, 2 + c:3 + c], in_=xtok(sl), axis=AX.X), sem=kb.sV)
            ph.wait("vector", kb.sV, v_rs)
            v_mean = ph.emit("vector", lambda e, c=c: e.tensor_scalar(
                out=kb.smf[:, 4 + c:5 + c], in0=kb.smf[:, 2 + c:3 + c],
                scalar1=1.0 / D, scalar2=None, op0=ALU.mult), sem=kb.sV)
            v_msq = ph.emit("vector", lambda e, c=c: e.tensor_scalar(
                out=kb.smf[:, 6 + c:7 + c], in0=kb.smf[:, 0 + c:1 + c],
                scalar1=1.0 / D, scalar2=None, op0=ALU.mult), sem=kb.sV)
            ph.wait("vector", kb.sV, v_mean)
            v_mm2 = ph.emit("vector", lambda e, c=c: e.tensor_tensor(
                out=kb.smf[:, 8 + c:9 + c], in0=kb.smf[:, 4 + c:5 + c],
                in1=kb.smf[:, 4 + c:5 + c], op=ALU.mult), sem=kb.sV)
            ph.wait("vector", kb.sV, v_mm2)
            v_var = ph.emit("vector", lambda e, c=c: e.tensor_tensor(
                out=kb.smf[:, 10 + c:11 + c], in0=kb.smf[:, 6 + c:7 + c],
                in1=kb.smf[:, 8 + c:9 + c], op=ALU.subtract), sem=kb.sV)
            ph.wait("scalar", kb.sV, v_var)
            a_sd = ph.emit("scalar", lambda e, c=c: e.activation(
                out=kb.smf[:, 12 + c:13 + c], in_=kb.smf[:, 10 + c:11 + c],
                func=AF.Sqrt, bias=kb.cc[:, CC_EPS_LN:CC_EPS_LN + 1]), sem=kb.sA)
            ph.wait("vector", kb.sA, a_sd)
            v_rec = ph.emit("vector", lambda e, c=c: e.reciprocal(
                out=kb.smf[:, 14 + c:15 + c], in_=kb.smf[:, 12 + c:13 + c]), sem=kb.sV)
            ph.wait("vector", kb.sV, v_rec)
            v_nb[tc] = ph.emit("vector", lambda e, c=c: e.scalar_tensor_tensor(
                out=kb.smf[:, 16 + c:17 + c], in0=kb.smf[:, 4 + c:5 + c],
                scalar=-1.0, in1=kb.smf[:, 14 + c:15 + c],
                op0=ALU.mult, op1=ALU.mult), sem=kb.sV)
            ph.wait("scalar", kb.sV, v_nb[tc])
            a_xn[tc] = ph.emit("scalar", lambda e, sl=sl, c=c: e.activation(
                out=xntk(sl), in_=xtok(sl), func=AF.Identity,
                scale=kb.smf[:, 14 + c:15 + c],
                bias=kb.smf[:, 16 + c:17 + c]), sem=kb.sA)
            if "dbg1" in kb.dump:
                ph.wait("sync", kb.sV, v_nb[tc])
                ph.dma("sync", kb.dbg1[tc, :, :], kb.smf[:, 0:24], kb.sS)
            for phase_sel in (0, 1):
                if phase_sel == 0:
                    ph.wait("tensor", kb.sd(f"x{sl}"), ld[tc])
                else:
                    ph.wait("tensor", kb.sA, a_xn[tc])
                for kc in range(DC):
                    b = tp_i % 4
                    if tp_i >= 4:
                        ph.wait("tensor", kb.sV, cp_order[tp_i - 4])
                    if phase_sel == 0:
                        m = ph.emit("tensor", lambda e, sl=sl, kc=kc, b=b: e.transpose(
                            kb.ps[b][:, 0:128], xtok(sl)[:, kc * 128:(kc + 1) * 128],
                            kb.idf[:]), sem=kb.sM)
                    else:
                        m = ph.emit("tensor", lambda e, sl=sl, kc=kc, b=b: e.transpose(
                            kb.psb[b][:, 0:128], xntk(sl)[:, kc * 128:(kc + 1) * 128],
                            kb.idb[:]), sem=kb.sM)
                    ph.wait("vector", kb.sM, m)
                    if tt > 0 and tp_i % (32 * 4) < 2:
                        ph.wait("vector", kb.sS, st_tt[tt - 1])
                    col = (tc % 4) * 128
                    if phase_sel == 0:
                        cv = ph.emit("vector", lambda e, kc=kc, b=b, col=col: e.tensor_copy(
                            out=ar32[:, kc * 512 + col: kc * 512 + col + 128],
                            in_=kb.ps[b][:, 0:128]), sem=kb.sV)
                    else:
                        cv = ph.emit("vector", lambda e, kc=kc, b=b, col=col: e.tensor_copy(
                            out=ar[:, XNST + kc * 512 + col: XNST + kc * 512 + col + 128],
                            in_=kb.psb[b][:, 0:128]), sem=kb.sV)
                    cp_order.append(cv)
                    tp_i += 1
                if phase_sel == 0:
                    m_xf[tc] = kb.sM.n
                else:
                    m_xn[tc] = kb.sM.n
            if tc % 4 == 3:
                ph.wait("sync", kb.sV, kb.sV.n)
                for kc in range(DC):
                    ph.dma("sync", kb.xf_s[tt, kc, :, :],
                           ar32[:, kc * 512:(kc + 1) * 512], kb.sS)
                    ph.dma("sync", kb.xn_s[tt, kc, :, :],
                           ar[:, XNST + kc * 512: XNST + (kc + 1) * 512], kb.sS)
                st_tt[tt] = kb.sS.n
        ph.wait("sync", kb.sS, kb.sS.n)
        ph.run()


    def p2(self):
        kb = self
        ph = Ph(kb, "p2")
        T, TT, TC = kb.T, kb.TT, kb.TC
        ar, wb = kb.ar, kb.wb

        def xnr(kc):
            return ar[:, kc * T:(kc + 1) * T]

        for kc in range(DC):
            for tt in range(TT):
                ph.dma("sync", ar[:, kc * T + tt * 512: kc * T + (tt + 1) * 512],
                       kb.xn_s[tt, kc, :, :], kb.sL)
        ld_xn = kb.sL.n

        def wt(kc):
            return wb[:, kc * 512:(kc + 1) * 512]

        def stg(s):
            return wb[:, 8192 + s * 512: 8192 + (s + 1) * 512]

        ph.wait("tensor", kb.sL, ld_xn)

        gld = {}
        m_kc = {}
        evac_b = {}
        st_s = [0] * 4
        gi = 0
        si = 0

        for tp in range(TT // 2):
            for g in range(8):
                for kc in range(DC):
                    if gi > 0:
                        ph.wait("gpsimd", kb.sM, m_kc[(gi - 1, kc)])
                    gld[(gi, kc)] = ph.dma(
                        "gpsimd", wt(kc),
                        kb.Wa[kc * 128:(kc + 1) * 128, g * 512:(g + 1) * 512],
                        kb.sd(f"wt{kc}"))
                for kc in range(DC):
                    ph.wait("tensor", kb.sd(f"wt{kc}"), gld[(gi, kc)])
                    for j in range(4):
                        for h2 in range(2):
                            b = j * 2 + h2
                            if kc == 0 and gi > 0:
                                ph.wait("tensor", kb.sA, evac_b.get((gi - 1, b), 0))
                            ph.emit("tensor", lambda e, kc=kc, j=j, h2=h2, b=b, tp=tp: e.matmul(
                                kb.ps[b][:],
                                wt(kc)[:, j * 128:(j + 1) * 128],
                                xnr(kc)[:, tp * 1024 + h2 * 512: tp * 1024 + (h2 + 1) * 512],
                                start=(kc == 0), stop=(kc == DC - 1)),
                                sem=kb.sM)
                    m_kc[(gi, kc)] = kb.sM.n
                for j in range(4):
                    nc_ = g * 4 + j
                    for h2 in range(2):
                        b = j * 2 + h2
                        s = si % 4
                        si += 1
                        ph.wait("scalar", kb.sM, m_kc[(gi, DC - 1)])
                        ph.wait("scalar", kb.sd(f"st{s}"), st_s[s])
                        if nc_ < 8:
                            ev = ph.emit("scalar", lambda e, b=b, s=s: e.activation(
                                out=stg(s), in_=kb.ps[b][:], func=AF.Identity,
                                scale=0.125), sem=kb.sA)
                        elif nc_ < 16:
                            ev = ph.emit("scalar", lambda e, b=b, s=s: e.activation(
                                out=stg(s), in_=kb.ps[b][:], func=AF.Identity), sem=kb.sA)
                        elif nc_ < 24:
                            ev = ph.emit("scalar", lambda e, b=b, s=s: e.activation(
                                out=stg(s), in_=kb.ps[b][:], func=AF.Identity,
                                scale=kb.q2s[:]), sem=kb.sA)
                        else:
                            ev = ph.emit("scalar", lambda e, b=b, s=s: e.activation(
                                out=stg(s), in_=kb.ps[b][:], func=AF.Identity), sem=kb.sA)
                        evac_b[(gi, b)] = ev
                        t0 = tp * 1024 + h2 * 512
                        c = nc_ % 8
                        h0, h1 = 2 * c, 2 * c + 1
                        if nc_ < 8:
                            dsts = [(kb.qs_s[h0, 0:64, t0:t0 + 512], 0),
                                    (kb.qs_s[h1, 0:64, t0:t0 + 512], 64)]
                        elif nc_ < 16:
                            dsts = [(kb.ks_s[h0, 0:64, t0:t0 + 512], 0),
                                    (kb.ks_s[h1, 0:64, t0:t0 + 512], 64)]
                        elif nc_ < 24:
                            dsts = [(kb.qs_s[h0, 64:128, t0:t0 + 512], 0),
                                    (kb.qs_s[h1, 64:128, t0:t0 + 512], 64)]
                        else:
                            dsts = [(kb.ks_s[h0, 64:128, t0:t0 + 512], 0),
                                    (kb.ks_s[h1, 64:128, t0:t0 + 512], 64)]
                        ph.wait("sync", kb.sA, ev)
                        for dst, r0 in dsts:
                            ph.dma("sync", dst, stg(s)[r0:r0 + 64, :], kb.sd(f"st{s}"))
                        st_s[s] = kb.sd(f"st{s}").n
                gi += 1

        for tcg in range(TC // 8):
            for dvg in range(4):
                for kc in range(DC):
                    ph.wait("gpsimd", kb.sM, m_kc[(gi - 1, kc)])
                    gld[(gi, kc)] = ph.dma(
                        "gpsimd", wt(kc),
                        kb.Wa[kc * 128:(kc + 1) * 128,
                              2 * D + dvg * 512: 2 * D + (dvg + 1) * 512],
                        kb.sd(f"wt{kc}"))
                for kc in range(DC):
                    ph.wait("tensor", kb.sd(f"wt{kc}"), gld[(gi, kc)])
                    for b in range(8):
                        tc = tcg * 8 + b
                        if kc == 0:
                            ph.wait("tensor", kb.sA, evac_b.get((gi - 1, b), 0))
                        ph.emit("tensor", lambda e, kc=kc, b=b, tc=tc: e.matmul(
                            kb.ps[b][:],
                            xnr(kc)[:, tc * 128:(tc + 1) * 128],
                            wt(kc)[:],
                            start=(kc == 0), stop=(kc == DC - 1)),
                            sem=kb.sM)
                    m_kc[(gi, kc)] = kb.sM.n
                for b in range(8):
                    tc = tcg * 8 + b
                    s = si % 4
                    si += 1
                    ph.wait("scalar", kb.sM, m_kc[(gi, DC - 1)])
                    ph.wait("scalar", kb.sd(f"st{s}"), st_s[s])
                    ev = ph.emit("scalar", lambda e, b=b, s=s: e.activation(
                        out=stg(s), in_=kb.ps[b][:], func=AF.Identity,
                        scale=0.8), sem=kb.sA)
                    evac_b[(gi, b)] = ev
                    ph.wait("sync", kb.sA, ev)
                    ph.dma("sync",
                           kb.vs_s[dvg * 4:(dvg + 1) * 4, tc, :, :].rearrange("h p d -> p h d"),
                           stg(s)[:].rearrange("p (h d) -> p h d", h=4), kb.sd(f"st{s}"))
                    st_s[s] = kb.sd(f"st{s}").n
                gi += 1
        for s in range(4):
            ph.wait("sync", kb.sd(f"st{s}"), kb.sd(f"st{s}").n)
        ph.run()

    def attn(self, name, causal, mask=False):
        kb = self
        ph = Ph(kb, name)
        T, TT, TC = kb.T, kb.TT, kb.TC
        ar, fb = kb.ar, kb.fb

        def qv(hp):
            return ar[:, hp * 16384: hp * 16384 + T]

        def kv(hp):
            return ar[:, hp * 16384 + 4096: hp * 16384 + 4096 + T]

        def vv(hp):
            return ar[:, hp * 16384 + 8192: hp * 16384 + 8192 + T]

        def trash(hp):
            return ar[:, hp * 16384 + 12288: hp * 16384 + 12288 + T]

        def pT(p):
            return ar[:, 32768 + p * 512: 32768 + (p + 1) * 512]

        def aost(p):
            return ar[:, 36864 + p * 512: 36864 + (p + 1) * 512]

        def rb(p):
            return kb.ar32[:, 19456 + p * 512: 19456 + (p + 1) * 512]

        NS = 3
        NPT = 8
        ld_head = [0] * NH
        done_head = [0] * NH
        ao_st = [0, 0]
        pt_mm = {}
        exp_of_bank = {}
        den_free = getattr(self, "_den_free", [0, 0])
        out_free = getattr(self, "_out_free", [0, 0])
        gtk = getattr(self, "_attn_gtk", 0)

        def issue_loads(h):
            hp = h % 2
            if h >= 2:
                ph.wait("sync", kb.sM, done_head[h - 2])
            lq = ph.dma("sync", qv(hp), kb.qs_s[h, :, :], kb.sd(f"aq{hp}"))
            lk = ph.dma("sync", kv(hp), kb.ks_s[h, :, :], kb.sd(f"ak{hp}"))
            lv = ph.dma(
                "sync", vv(hp).rearrange("p (tc d) -> p tc d", d=128),
                kb.vs_s[h, :, :, :].rearrange("tc p d -> p tc d"), kb.sd(f"av{hp}"))
            ld_head[h] = (lq, lk, lv)

        if mask:
            for j in range(4):
                ph.dma("sync", kb.msk[:, j * 512:(j + 1) * 512],
                       kb.c_mask[j, :, :], kb.sd("msk"))
            self._msk_ld = kb.sd("msk").n
        issue_loads(0)
        for h in range(NH):
            hp = h % 2
            if h + 1 < NH:
                issue_loads(h + 1)
            for which, src in (("q", qv(hp)), ("k", kv(hp))):
                if h >= 2:
                    ph.wait("scalar", kb.sM, done_head[h - 2])
                ph.wait("scalar", kb.sd(f"aq{hp}"), ld_head[h][0])
                ph.wait("scalar", kb.sd(f"ak{hp}"), ld_head[h][1])
                ph.wait("scalar", kb.sM, self._trash_free)
                a = ph.emit("scalar", lambda e, src=src, hp=hp: e.activation(
                    out=trash(hp), in_=src, func=AF.Square), sem=kb.sA)
                base = 72 if which == "q" else 80
                last_rd = max(self._ps7_free, self._ps7_bcast_free)
                for i in range(TT):
                    ph.wait("tensor", kb.sA, a)
                    if last_rd:
                        ph.wait("tensor", kb.sV, last_rd)
                    mmv = ph.emit("tensor", lambda e, i=i, hp=hp: e.matmul(
                        kb.ps[7][0:1, 0:512], kb.onesb[:, 0:1],
                        trash(hp)[:, i * 512:(i + 1) * 512],
                        start=True, stop=True), sem=kb.sM)
                    ph.wait("vector", kb.sM, mmv)
                    last_rd = ph.emit("vector", lambda e, i=i, base=base: e.reduce_max(
                        out=kb.smf[0:1, base + i: base + i + 1],
                        in_=kb.ps[7][0:1, 0:512], axis=AX.X), sem=kb.sV)
                self._trash_free = kb.sM.n
                self._ps7_free = last_rd
                ph.wait("vector", kb.sV, last_rd)
                dst = 88 if which == "q" else 89
                ph.emit("vector", lambda e, base=base, dst=dst: e.reduce_max(
                    out=kb.smf[0:1, dst:dst + 1], in_=kb.smf[0:1, base:base + TT],
                    axis=AX.X), sem=kb.sV)
            v_mx = kb.sV.n
            ph.wait("vector", kb.sV, v_mx)
            v_t = ph.emit("vector", lambda e: e.tensor_tensor(
                out=kb.smf[0:1, 90:91], in0=kb.smf[0:1, 88:89],
                in1=kb.smf[0:1, 89:90], op=ALU.mult), sem=kb.sV)
            ph.wait("scalar", kb.sV, v_t)
            a_sq2 = ph.emit("scalar", lambda e: e.activation(
                out=kb.smf[0:1, 91:92], in_=kb.smf[0:1, 90:91], func=AF.Sqrt), sem=kb.sA)
            ph.wait("vector", kb.sA, a_sq2)
            v_ng = ph.emit("vector", lambda e: e.tensor_scalar(
                out=kb.smf[0:1, 92:93], in0=kb.smf[0:1, 91:92],
                scalar1=-1.02, scalar2=-0.25, op0=ALU.mult, op1=ALU.add), sem=kb.sV)
            ph.wait("tensor", kb.sV, v_ng)
            ph.wait("tensor", kb.sV, self._ps7_bcast_free)
            m_bc = ph.emit("tensor", lambda e: e.matmul(
                kb.ps[7][:, 0:1], kb.onesr[:, :], kb.smf[0:1, 92:93],
                start=True, stop=True), sem=kb.sM)
            ph.wait("vector", kb.sM, m_bc)
            ph.wait("vector", kb.sA, self._nub_users[hp])
            v_nub = ph.emit("vector", lambda e, hp=hp: e.tensor_copy(
                out=kb.nub[:, hp:hp + 1], in_=kb.ps[7][:, 0:1]), sem=kb.sV)
            self._ps7_free = v_nub
            ph.wait("scalar", kb.sV, v_nub)

            ph.wait("tensor", kb.sd(f"aq{hp}"), ld_head[h][0])
            ph.wait("tensor", kb.sd(f"ak{hp}"), ld_head[h][1])
            ph.wait("tensor", kb.sd(f"av{hp}"), ld_head[h][2])
            for tqt in range(TT):
                par = tqt % 2
                ob = 3 + par
                db = 5 + par
                n_tk = 4 * (tqt + 1) if causal else TC
                first_tk = True
                LOOK = 2
                pending = []

                def emit_S(tkc, gi):
                    b = gi % NS
                    p = gi % NPT
                    ph.wait("tensor", kb.sA, exp_of_bank.get(b, 0))
                    m_s = ph.emit("tensor", lambda e, b=b, tkc=tkc, tqt=tqt, hp=hp: e.matmul(
                        kb.ps[b][:], kv(hp)[:, tkc * 128:(tkc + 1) * 128],
                        qv(hp)[:, tqt * 512:(tqt + 1) * 512],
                        start=True, stop=True), sem=kb.sM)
                    ph.wait("scalar", kb.sM, max(m_s, pt_mm.get(p, 0)))
                    a_exp = ph.emit("scalar", lambda e, b=b, p=p, hp=hp: e.activation(
                        out=pT(p), in_=kb.ps[b][:], func=AF.Exp,
                        bias=kb.nub[:, hp:hp + 1]), sem=kb.sA)
                    exp_of_bank[b] = a_exp
                    pt_sem, pt_ready = kb.sA, a_exp
                    if causal and mask and tkc >= 4 * tqt:
                        j = tkc - 4 * tqt
                        ph.wait("vector", kb.sd("msk"), self._msk_ld)
                        ph.wait("vector", kb.sA, a_exp)
                        v_m = ph.emit("vector", lambda e, p=p, j=j: e.tensor_tensor(
                            out=pT(p), in0=pT(p),
                            in1=kb.msk[:, j * 512:(j + 1) * 512], op=ALU.mult),
                            sem=kb.sV)
                        pt_sem, pt_ready = kb.sV, v_m
                    pending.append((tkc, p, pt_sem, pt_ready))

                def emit_DV(entry):
                    nonlocal first_tk
                    tkc, p, pt_sem, pt_ready = entry
                    ph.wait("tensor", pt_sem, pt_ready)
                    if first_tk:
                        ph.wait("tensor", kb.sV, den_free[par])
                    ph.emit("tensor", lambda e, p=p, db=db, st=first_tk,
                            sp=(tkc == n_tk - 1): e.matmul(
                        kb.ps[db][0:1, 0:512], kb.onesb[:, 0:1], pT(p),
                        start=st, stop=sp), sem=kb.sM)
                    if first_tk:
                        ph.wait("tensor", kb.sV, out_free[par])
                    m_v = ph.emit("tensor", lambda e, p=p, tkc=tkc, hp=hp, ob=ob,
                                  st=first_tk, sp=(tkc == n_tk - 1): e.matmul(
                        kb.ps[ob][:], vv(hp)[:, tkc * 128:(tkc + 1) * 128], pT(p),
                        start=st, stop=sp), sem=kb.sM)
                    pt_mm[p] = m_v
                    first_tk = False

                for tkc in range(n_tk):
                    emit_S(tkc, gtk)
                    gtk += 1
                    if len(pending) > LOOK:
                        emit_DV(pending.pop(0))
                while pending:
                    emit_DV(pending.pop(0))
                m_all = kb.sM.n
                ph.wait("vector", kb.sM, m_all)
                v_rc = ph.emit("vector", lambda e, db=db, par=par: e.reciprocal(
                    out=kb.row[0:1, par * 512:(par + 1) * 512],
                    in_=kb.ps[db][0:1, 0:512]), sem=kb.sV)
                den_free[par] = v_rc
                ph.wait("tensor", kb.sV, v_rc)
                ph.wait("tensor", kb.sV, self._ps7_bcast_free)
                m_b2 = ph.emit("tensor", lambda e, par=par: e.matmul(
                    kb.ps[7][:, 0:512], kb.onesr[:, :],
                    kb.row[0:1, par * 512:(par + 1) * 512],
                    start=True, stop=True), sem=kb.sM)
                ph.wait("vector", kb.sM, m_b2)
                v_cb = ph.emit("vector", lambda e, par=par: e.tensor_copy(
                    out=rb(par), in_=kb.ps[7][:, 0:512]), sem=kb.sV)
                self._ps7_bcast_free = v_cb
                sp_ = par
                ph.wait("vector", kb.sd(f"ao{sp_}"), ao_st[sp_])
                v_o = ph.emit("vector", lambda e, sp_=sp_, ob=ob: e.scalar_tensor_tensor(
                    out=aost(sp_), in0=kb.ps[ob][:], scalar=1.0, in1=rb(sp_),
                    op0=ALU.mult, op1=ALU.mult), sem=kb.sV)
                out_free[par] = v_o
                ph.wait("gpsimd", kb.sV, v_o)
                ph.dma("gpsimd", kb.ao_s[h, :, tqt * 512:(tqt + 1) * 512],
                       aost(sp_), kb.sd(f"ao{sp_}"))
                ao_st[sp_] = kb.sd(f"ao{sp_}").n
            done_head[h] = kb.sM.n
            self._nub_users[hp] = kb.sA.n
        for sp_ in range(2):
            ph.wait("gpsimd", kb.sd(f"ao{sp_}"), kb.sd(f"ao{sp_}").n)
        self._den_free = den_free
        self._out_free = out_free
        self._attn_gtk = gtk
        ph.run()

    def _attn_init(self):
        if not hasattr(self, "_trash_free"):
            self._trash_free = 0
            self._ps7_free = 0
            self._ps7_bcast_free = 0
            self._nub_users = [0, 0]

    def p3(self):
        self._attn_init()
        self.attn("p3", causal=False)


    def p5(self):
        kb = self
        ph = Ph(kb, "p5")
        fb, wb, ib = kb.fb, kb.wb, kb.ib
        ar32 = kb.ar32
        F_SC = 0

        def tS(q):
            return ar32[:, q * 5120: q * 5120 + 1024]

        def tR1(q):
            return ar32[:, q * 5120 + 1024: q * 5120 + 2048]

        def tR2(q):
            return ar32[:, q * 5120 + 2048: q * 5120 + 3072]

        def tM0(q):
            return ar32[:, q * 5120 + 3072: q * 5120 + 4096]

        def tMG(q):
            return ar32[:, q * 5120 + 4096: q * 5120 + 5120]

        ari = kb.ar.bitcast(I32)

        def nibN(q2):
            return ari[:, 20480 + q2 * 1024: 20480 + (q2 + 1) * 1024]

        def nibM(q2):
            return ari[:, 20480 + (2 + q2) * 1024: 20480 + (3 + q2) * 1024]

        def vbf(vp):
            return wb[:, 1024 + vp * 2048: 1024 + (vp + 1) * 2048]

        def ost(vp):
            return wb[:, 5120 + vp * 2048: 5120 + (vp + 1) * 2048]

        pieces = []
        for rc in range(128):
            pieces.append((kb.g_pk[rc * 128:(rc + 1) * 128, :],
                           kb.g_sc[rc * 128:(rc + 1) * 128, :],
                           kb.WgT_s[rc, :, :]))
        for rc in range(128):
            pieces.append((kb.u_pk[rc * 128:(rc + 1) * 128, :],
                           kb.u_sc[rc * 128:(rc + 1) * 128, :],
                           kb.WuT_s[rc, :, :]))
        for rc in range(16):
            for pc in range(8):
                dst4 = kb.WdT2[rc // 4, pc * 16:(pc + 1) * 16, :,
                               (rc % 4) * 128:(rc % 4) * 128 + 128]
                pieces.append((kb.d_pk[rc * 128:(rc + 1) * 128, pc * 1024:(pc + 1) * 1024],
                               kb.d_sc[rc * 128:(rc + 1) * 128, pc * 64:(pc + 1) * 64],
                               ("wd", dst4)))
        if "dbg5" in kb.dump:
            pieces = pieces[:1]
        import os as _os
        if _os.environ.get("P5_NPIECE"):
            pieces = pieces[:int(_os.environ["P5_NPIECE"])]
        NP = len(pieces)
        NK = 2 * NP

        ld_pk = {}
        ld_sc = {}
        vm = {}
        vr2 = {}
        vmag = {}
        vs_ = {}
        pval = {}
        pout = {}
        p_m0_hist = {}
        txd = {}
        st_p = {}

        def issue_piece_loads(pi):
            if pi >= NP:
                return
            pp = pi % 2
            pk, sc, dst = pieces[pi]
            if pi >= 2:
                ph.wait("sync", kb.sV, vm.get(2 * (pi - 2) + 1, 0))
                ph.wait("sync", kb.sV, pout.get(2 * (pi - 2) + 1, 0))
            ld_pk[pi] = ph.dma("sync", ib[:, pp * 1024:(pp + 1) * 1024], pk,
                               kb.sd(f"pk{pp}"))
            ld_sc[pi] = ph.dma("sync", fb[:, F_SC + pp * 64: F_SC + (pp + 1) * 64],
                               sc, kb.sd(f"sc{pp}"))

        def stageDA(k):
            pi, pl = k // 2, k % 2
            pp = pi % 2
            q2 = k % 2
            q = k % 4
            ph.wait("vector", kb.sd(f"pk{pp}"), ld_pk[pi])
            ph.wait("vector", kb.sA, vr2.get(k - 2, 0))
            if pl == 0:
                vn = ph.emit("vector", lambda e, pp=pp, q2=q2: e.tensor_scalar(
                    out=nibN(q2), in0=ib[:, pp * 1024:(pp + 1) * 1024],
                    scalar1=15, scalar2=None, op0=ALU.bitwise_and), sem=kb.sV)
                vm[k] = ph.emit("vector", lambda e, pp=pp, q2=q2: e.tensor_scalar(
                    out=nibM(q2), in0=ib[:, pp * 1024:(pp + 1) * 1024],
                    scalar1=7, scalar2=None, op0=ALU.bitwise_and), sem=kb.sV)
            else:
                vn = ph.emit("vector", lambda e, pp=pp, q2=q2: e.tensor_scalar(
                    out=nibN(q2), in0=ib[:, pp * 1024:(pp + 1) * 1024],
                    scalar1=4, scalar2=15, op0=ALU.logical_shift_right,
                    op1=ALU.bitwise_and), sem=kb.sV)
                vm[k] = ph.emit("vector", lambda e, pp=pp, q2=q2: e.tensor_scalar(
                    out=nibM(q2), in0=ib[:, pp * 1024:(pp + 1) * 1024],
                    scalar1=4, scalar2=7, op0=ALU.logical_shift_right,
                    op1=ALU.bitwise_and), sem=kb.sV)
            ph.wait("scalar", kb.sV, vm[k])
            ph.wait("scalar", kb.sP, pval.get(k - 4, 0))
            ph.wait("scalar", kb.sP, p_m0_hist.get(k - 4, 0))
            vs_[k] = ph.emit("scalar", lambda e, q=q, q2=q2: e.activation(
                out=tS(q), in_=nibN(q2), func=AF.Sign,
                scale=-0.5, bias=kb.cc[:, CC_3p75:CC_3p75 + 1]), sem=kb.sA)
            ph.emit("scalar", lambda e, q=q, q2=q2: e.activation(
                out=tR1(q), in_=nibM(q2), func=AF.Relu, scale=0.5,
                bias=kb.cc[:, CC_m2:CC_m2 + 1]), sem=kb.sA)
            vr2[k] = ph.emit("scalar", lambda e, q=q, q2=q2: e.activation(
                out=tR2(q), in_=nibM(q2), func=AF.Relu,
                bias=kb.cc[:, CC_m6:CC_m6 + 1]), sem=kb.sA)

        def stageD2(k):
            if k < 0 or k >= NK:
                return
            q = k % 4
            q2 = k % 2
            ph.wait("gpsimd", kb.sA, vr2[k])
            ph.wait("gpsimd", kb.sV, pout.get(k - 4, 0))
            p_m0 = ph.emit("gpsimd", lambda e, q=q: e.tensor_tensor(
                out=tM0(q), in0=tR1(q), in1=tR2(q), op=ALU.add), sem=kb.sP)
            p_m0_hist[k] = p_m0
            ph.wait("vector", kb.sP, p_m0)
            ph.wait("vector", kb.sP, pval.get(k - 4, 0))
            vmag[k] = ph.emit("vector", lambda e, q=q, q2=q2: e.scalar_tensor_tensor(
                out=tMG(q), in0=nibM(q2), scalar=0.5, in1=tM0(q),
                op0=ALU.mult, op1=ALU.add), sem=kb.sV)

        def stageG(k):
            if k < 0 or k >= NK:
                return
            pi, pl = k // 2, k % 2
            pp = pi % 2
            q = k % 4
            ph.wait("gpsimd", kb.sV, vmag[k])
            ph.wait("gpsimd", kb.sA, vs_[k])
            pval[k] = ph.emit("gpsimd", lambda e, q=q: e.tensor_tensor(
                out=tM0(q), in0=tMG(q), in1=tS(q), op=ALU.mult), sem=kb.sP)
            ph.wait("vector", kb.sP, pval[k])
            ph.wait("vector", kb.sd(f"sc{pp}"), ld_sc[pi])
            if pi >= 2:
                ph.wait("vector", kb.sd(f"tx{pp}"), txd.get(pi - 2, 0))
            pout[k] = ph.emit("vector", lambda e, pp=pp, pl=pl, q=q: e.tensor_tensor(
                out=vbf(pp).rearrange("p (b j two) -> p b j two", b=64, two=2)[:, :, :, pl],
                in0=tM0(q).rearrange("p (b j) -> p b j", b=64),
                in1=fb[:, F_SC + pp * 64: F_SC + (pp + 1) * 64].broadcast_to([128, 64, 16]),
                op=ALU.mult), sem=kb.sV)

        def finalize_piece(pi):
            if pi < 0 or pi >= NP:
                return
            pp = pi % 2
            dst = pieces[pi][2]
            ph.wait("sync", kb.sV, pout[2 * pi + 1])
            if pi >= 2:
                ph.wait("sync", kb.sd(f"os{pp}"), st_p.get(pi - 2, 0))
            for j in range(16):
                ph.dma("sync", ost(pp)[:, j * 128:(j + 1) * 128],
                       vbf(pp)[:, j * 128:(j + 1) * 128],
                       kb.sd(f"tx{pp}"), transpose=True)
            txd[pi] = kb.sd(f"tx{pp}").n
            ph.wait("scalar", kb.sd(f"tx{pp}"), txd[pi])
            if isinstance(dst, tuple):
                st_p[pi] = ph.dma("scalar", dst[1].rearrange("n p m -> p n m"),
                                  ost(pp).rearrange("p (n m) -> p n m", n=16),
                                  kb.sd(f"os{pp}"))
            else:
                st_p[pi] = ph.dma("scalar", dst, ost(pp), kb.sd(f"os{pp}"))

        import os
        PIPE = os.environ.get("P5_SERIAL", "") == ""
        issue_piece_loads(0)
        issue_piece_loads(1)
        if PIPE:
            for k in range(NK + 3):
                if k < NK:
                    stageDA(k)
                stageD2(k - 1)
                stageG(k - 2)
                if k >= 3 and k % 2 == 1:
                    issue_piece_loads((k + 1) // 2)
                if k >= 3 and (k - 3) % 2 == 0:
                    finalize_piece((k - 3) // 2)
        else:
            for k in range(NK):
                stageDA(k)
                stageD2(k)
                stageG(k)
                if k % 2 == 1:
                    issue_piece_loads(k // 2 + 2)
                if "dbg5" in kb.dump and k <= 1:
                    ph.wait("sync", kb.sV, kb.sV.n)
                    ph.wait("sync", kb.sA, kb.sA.n)
                    ph.dma("sync", kb.dbg5[4 * k + 0, :, :], nibN(k % 2).bitcast(F32), kb.sS)
                    ph.dma("sync", kb.dbg5[4 * k + 1, :, :], nibM(k % 2).bitcast(F32), kb.sS)
                    ph.dma("sync", kb.dbg5[4 * k + 2, :, :], tMG(k % 4), kb.sS)
                    ph.dma("sync", kb.dbg5[4 * k + 3, :, :], tM0(k % 4), kb.sS)
                    if k == 1:
                        ph.dma("sync", kb.dbg5b[:, :], vbf(0), kb.sS)
                    ph.wait("sync", kb.sS, kb.sS.n)
                if k % 2 == 1:
                    finalize_piece(k // 2)
        for pp in range(2):
            ph.wait("scalar", kb.sd(f"os{pp}"), kb.sd(f"os{pp}").n)
        ph.run()

    def p6(self):
        kb = self
        ph = Ph(kb, "p6")
        T = kb.T
        NTP = T // 1024
        ar, fb, wb = kb.ar, kb.fb, kb.wb

        XN6, AO6, IST, IRH = 0, 16384, 32768, 34816

        def xn6(kc):
            return ar[:, XN6 + kc * 1024: XN6 + (kc + 1) * 1024]

        def ao6(kc):
            return ar[:, AO6 + kc * 1024: AO6 + (kc + 1) * 1024]

        def ist(s):
            return ar[:, IST + s * 1024: IST + (s + 1) * 1024]

        def irh(s):
            return ar[:, IRH + s * 2048: IRH + (s + 1) * 2048]

        def wgt(s):
            return wb[:, s * 2048:(s + 1) * 2048]

        def wut(s):
            return wb[:, 4096 + s * 2048: 4096 + (s + 1) * 2048]

        def wpt(s):
            return wb[:, 8192 + s * 512: 8192 + (s + 1) * 512]

        def wdt(s):
            return wb[:, 9216 + s * 1024: 9216 + (s + 1) * 1024]

        wg_free = [0, 0]
        wu_free = [0, 0]
        wp_free = [0, 0]
        wd_free = [0, 0]
        ir_free = [0, 0]
        ev_free = [0, 0]
        v_int_prev = {}
        ist_st = [0, 0]
        c_free = 0
        b_free = 0
        cb_free = 0
        x0_free = [0, 0]
        x1_free = {}

        for tp in range(NTP):
            for kc in range(DC):
                for half in range(2):
                    ph.dma("sync", xn6(kc)[:, half * 512:(half + 1) * 512],
                           kb.xn_s[tp * 2 + half, kc, :, :], kb.sd("xn6"))
            ld_xn6 = kb.sd("xn6").n
            ph.wait("tensor", kb.sd("xn6"), ld_xn6)
            ph.wait("tensor", kb.sV, c_free)
            for nc_ in range(128):
                gi = tp * 128 + nc_
                s = gi % 2
                r = nc_ % 2
                ph.wait("sync", kb.sM, wg_free[s])
                ld_wg = ph.dma("sync", wgt(s), kb.WgT_s[nc_, :, :], kb.sd(f"g6{s}"))
                ph.wait("sync", kb.sM, wu_free[s])
                ld_wu = ph.dma("sync", wut(s), kb.WuT_s[nc_, :, :], kb.sd(f"u6{s}"))
                ph.wait("tensor", kb.sd(f"g6{s}"), ld_wg)
                ph.wait("tensor", kb.sV, ev_free[r])
                for kc in range(DC):
                    for hf in range(2):
                        ph.emit("tensor", lambda e, kc=kc, hf=hf, s=s, r=r: e.matmul(
                            kb.ps[hf + 4 * r][:], wgt(s)[:, kc * 128:(kc + 1) * 128],
                            xn6(kc)[:, hf * 512:(hf + 1) * 512],
                            start=(kc == 0), stop=(kc == DC - 1)), sem=kb.sM)
                m_g = kb.sM.n
                wg_free[s] = m_g
                ph.wait("tensor", kb.sd(f"u6{s}"), ld_wu)
                for kc in range(DC):
                    for hf in range(2):
                        ph.emit("tensor", lambda e, kc=kc, hf=hf, s=s, r=r: e.matmul(
                            kb.ps[2 + hf + 4 * r][:], wut(s)[:, kc * 128:(kc + 1) * 128],
                            xn6(kc)[:, hf * 512:(hf + 1) * 512],
                            start=(kc == 0), stop=(kc == DC - 1)), sem=kb.sM)
                m_u = kb.sM.n
                wu_free[s] = m_u
                ssl = gi % 2
                v_int = 0
                for hf in range(2):
                    gb = kb.ps[hf + 4 * r]
                    ub = kb.ps[2 + hf + 4 * r]
                    A = fb[:, hf * 512:(hf + 1) * 512]
                    B = fb[:, 1024 + hf * 512: 1024 + (hf + 1) * 512]
                    ph.wait("scalar", kb.sM, m_g)
                    ph.wait("scalar", kb.sV, v_int_prev.get(hf, 0))
                    a_sq = ph.emit("scalar", lambda e, gb=gb, A=A: e.activation(
                        out=A, in_=gb[:], func=AF.Square), sem=kb.sA)
                    ph.wait("vector", kb.sA, a_sq)
                    ph.emit("vector", lambda e, A=A, B=B: e.tensor_scalar(
                        out=B, in0=A, scalar1=0.0356774081, scalar2=0.7978845608,
                        op0=ALU.mult, op1=ALU.add))
                    v_u = ph.emit("vector", lambda e, A=A, B=B, gb=gb: e.tensor_tensor(
                        out=A, in0=B, in1=gb[:], op=ALU.mult), sem=kb.sV)
                    ph.wait("scalar", kb.sV, v_u)
                    a_th = ph.emit("scalar", lambda e, A=A, B=B: e.activation(
                        out=B, in_=A, func=AF.Tanh), sem=kb.sA)
                    ph.wait("vector", kb.sA, a_th)
                    ph.wait("vector", kb.sM, m_u)
                    ph.emit("vector", lambda e, A=A, B=B, gb=gb: e.scalar_tensor_tensor(
                        out=A, in0=B, scalar=1.0, in1=gb[:],
                        op0=ALU.add, op1=ALU.mult))
                    if hf == 0:
                        ph.wait("vector", kb.sd(f"i6{ssl}"), ist_st[ssl])
                    v_int = ph.emit("vector", lambda e, A=A, ub=ub, ssl=ssl, hf=hf:
                                    e.scalar_tensor_tensor(
                        out=ist(ssl)[:, hf * 512:(hf + 1) * 512], in0=A, scalar=0.5,
                        in1=ub[:], op0=ALU.mult, op1=ALU.mult), sem=kb.sV)
                    v_int_prev[hf] = v_int
                ev_free[r] = v_int
                ph.wait("scalar", kb.sV, v_int)
                ist_st[ssl] = ph.dma("scalar", kb.int_s[nc_, :, tp * 1024:(tp + 1) * 1024],
                                     ist(ssl), kb.sd(f"i6{ssl}"))
            int_done = [kb.sd("i60").n, kb.sd("i61").n]
            b_free = kb.sV.n
            for kc in range(DC):
                ph.dma("sync", ao6(kc), kb.ao_s[kc, :, tp * 1024:(tp + 1) * 1024],
                       kb.sd("ao6"))
            ld_ao6 = kb.sd("ao6").n
            ph.wait("tensor", kb.sd("ao6"), ld_ao6)
            ph.wait("tensor", kb.sd("i60"), int_done[0])
            ph.wait("tensor", kb.sd("i61"), int_done[1])
            ph.wait("sync", kb.sd("i60"), int_done[0])
            ph.wait("sync", kb.sd("i61"), int_done[1])
            for og in range(4):
                for kc in range(DC):
                    s = kc % 2
                    ph.wait("gpsimd", kb.sM, wp_free[s])
                    ld_wp = ph.dma("gpsimd", wpt(s),
                                   kb.Wp[kc * 128:(kc + 1) * 128,
                                         og * 512:(og + 1) * 512], kb.sd(f"p6{s}"))
                    ph.wait("tensor", kb.sd(f"p6{s}"), ld_wp)
                    if kc == 0:
                        ph.wait("tensor", kb.sV, cb_free)
                    for j in range(4):
                        for hf in range(2):
                            ph.emit("tensor", lambda e, kc=kc, j=j, hf=hf, s=s: e.matmul(
                                kb.ps[j * 2 + hf][:], wpt(s)[:, j * 128:(j + 1) * 128],
                                ao6(kc)[:, hf * 512:(hf + 1) * 512],
                                start=(kc == 0), stop=False), sem=kb.sM)
                    wp_free[s] = kb.sM.n
                for np_ in range(64):
                    n0 = np_ * 2
                    s = np_ % 2
                    ph.wait("gpsimd", kb.sM, wd_free[s])
                    ld_wd = ph.dma("gpsimd", wdt(s).rearrange("p (n c) -> p n c", n=2),
                                   kb.WdT2[og, n0:n0 + 2, :, :].rearrange("n p c -> p n c"),
                                   kb.sd(f"d6{s}"))
                    ph.wait("sync", kb.sM, ir_free[s])
                    ld_ir = ph.dma("sync", irh(s).rearrange("p (n c) -> p n c", n=2),
                                   kb.int_s[n0:n0 + 2, :, tp * 1024:(tp + 1) * 1024]
                                   .rearrange("n p c -> p n c"),
                                   kb.sd(f"r6{s}"))
                    ph.wait("tensor", kb.sd(f"d6{s}"), ld_wd)
                    ph.wait("tensor", kb.sd(f"r6{s}"), ld_ir)
                    for dn in range(2):
                        n = n0 + dn
                        for j in range(4):
                            for hf in range(2):
                                ph.emit("tensor", lambda e, dn=dn, j=j, hf=hf, s=s,
                                        last=(n == 127): e.matmul(
                                    kb.ps[j * 2 + hf][:],
                                    wdt(s)[:, dn * 512 + j * 128: dn * 512 + (j + 1) * 128],
                                    irh(s)[:, dn * 1024 + hf * 512: dn * 1024 + (hf + 1) * 512],
                                    start=False, stop=last), sem=kb.sM)
                    wd_free[s] = ir_free[s] = kb.sM.n
                m_c = kb.sM.n
                for j in range(4):
                    oc = og * 4 + j
                    xsl = j % 2
                    ph.wait("sync", kb.sV, x0_free[xsl])
                    lx0a = ph.dma("sync", fb[:, 2048 + xsl * 1024: 2048 + xsl * 1024 + 512],
                                  kb.xf_s[tp * 2, oc, :, :], kb.sd(f"x6{xsl}"))
                    lx0b = ph.dma("sync", fb[:, 2048 + xsl * 1024 + 512: 2048 + (xsl + 1) * 1024],
                                  kb.xf_s[tp * 2 + 1, oc, :, :], kb.sd(f"x6{xsl}"))
                    for hf in range(2):
                        ph.wait("vector", kb.sM, m_c)
                        ph.wait("vector", kb.sd(f"x6{xsl}"), lx0b)
                        ph.wait("vector", kb.sd(f"y6{xsl}"), x1_free.get((xsl, hf), 0))
                        v_x1 = ph.emit("vector", lambda e, j=j, hf=hf, xsl=xsl: e.tensor_tensor(
                            out=fb[:, 4096 + xsl * 1024 + hf * 512: 4096 + xsl * 1024 + (hf + 1) * 512],
                            in0=kb.ps[j * 2 + hf][:],
                            in1=fb[:, 2048 + xsl * 1024 + hf * 512: 2048 + xsl * 1024 + (hf + 1) * 512],
                            op=ALU.add), sem=kb.sV)
                        ph.wait("scalar", kb.sV, v_x1)
                        x1_free[(xsl, hf)] = ph.dma(
                            "scalar", kb.xf_s[tp * 2 + hf, oc, :, :],
                            fb[:, 4096 + xsl * 1024 + hf * 512: 4096 + xsl * 1024 + (hf + 1) * 512],
                            kb.sd(f"y6{xsl}"))
                    x0_free[xsl] = kb.sV.n
                cb_free = kb.sV.n
            c_free = kb.sV.n
        for t_ in ("y60", "y61", "i60", "i61"):
            ph.wait("scalar", kb.sd(t_), kb.sd(t_).n)
        ph.run()


    def p7(self):
        kb = self
        self._p7_rows_free = 0
        ph = Ph(kb, "p7")
        ar32 = kb.ar32
        X7, SQ7, RB7, NB7 = 0, 8192, 9216, 9728
        XST = 20480

        def x7(kc):
            return ar32[:, X7 + kc * 512: X7 + (kc + 1) * 512]

        m_rows_prev = 0
        v_apply_prev = 0
        st_xn = [0, 0]
        a_sq_free = 0
        for tt in range(kb.TT):
            for kc in range(DC):
                if tt > 0 and kc == 0:
                    ph.wait("sync", kb.sV, v_apply_prev)
                ph.dma("sync", x7(kc), kb.xf_s[tt, kc, :, :], kb.sd("x7"))
            ld_x7 = kb.sd("x7").n
            a_sq = 0
            for kc in range(DC):
                sl = kc % 2
                ph.wait("scalar", kb.sd("x7"), ld_x7)
                if kc >= 2:
                    ph.wait("scalar", kb.sM, m_sq_mm[sl])
                elif tt > 0:
                    ph.wait("scalar", kb.sM, m_rows_prev)
                a_sq = ph.emit("scalar", lambda e, kc=kc, sl=sl: e.activation(
                    out=ar32[:, SQ7 + sl * 512: SQ7 + (sl + 1) * 512],
                    in_=x7(kc), func=AF.Square), sem=kb.sA)
                ph.wait("tensor", kb.sd("x7"), ld_x7)
                ph.wait("tensor", kb.sA, a_sq)
                if kc == 0:
                    ph.wait("tensor", kb.sV, self._p7_rows_free)
                ph.emit("tensor", lambda e, kc=kc: e.matmul(
                    kb.ps[0][0:1, 0:512], kb.onesc[:, 0:1], x7(kc),
                    start=(kc == 0), stop=(kc == DC - 1)), sem=kb.sM)
                mm_sq = ph.emit("tensor", lambda e, kc=kc, sl=sl: e.matmul(
                    kb.ps[1][0:1, 0:512], kb.onesc[:, 0:1],
                    ar32[:, SQ7 + sl * 512: SQ7 + (sl + 1) * 512],
                    start=(kc == 0), stop=(kc == DC - 1)), sem=kb.sM)
                if kc == 0:
                    m_sq_mm = [0, 0]
                m_sq_mm[sl] = mm_sq
            m_rows = kb.sM.n
            m_rows_prev = m_rows
            r = kb.row
            ph.wait("vector", kb.sM, m_rows)
            v1 = ph.emit("vector", lambda e: e.tensor_scalar(
                out=r[0:1, 512:1024], in0=kb.ps[0][0:1, 0:512],
                scalar1=1.0 / D, scalar2=None, op0=ALU.mult), sem=kb.sV)
            v2 = ph.emit("vector", lambda e: e.tensor_scalar(
                out=r[0:1, 1024:1536], in0=kb.ps[1][0:1, 0:512],
                scalar1=1.0 / D, scalar2=None, op0=ALU.mult), sem=kb.sV)
            ph.wait("vector", kb.sV, v2)
            v3 = ph.emit("vector", lambda e: e.tensor_tensor(
                out=r[0:1, 1536:2048], in0=r[0:1, 512:1024],
                in1=r[0:1, 512:1024], op=ALU.mult), sem=kb.sV)
            ph.wait("vector", kb.sV, v3)
            v4 = ph.emit("vector", lambda e: e.tensor_tensor(
                out=r[0:1, 1024:1536], in0=r[0:1, 1024:1536],
                in1=r[0:1, 1536:2048], op=ALU.subtract), sem=kb.sV)
            ph.wait("scalar", kb.sV, v4)
            a_sd = ph.emit("scalar", lambda e: e.activation(
                out=r[0:1, 1536:2048], in_=r[0:1, 1024:1536], func=AF.Sqrt,
                bias=kb.cc[0:1, CC_EPS_LN:CC_EPS_LN + 1]), sem=kb.sA)
            ph.wait("vector", kb.sA, a_sd)
            v5 = ph.emit("vector", lambda e: e.reciprocal(
                out=r[0:1, 1024:1536], in_=r[0:1, 1536:2048]), sem=kb.sV)
            ph.wait("vector", kb.sV, v5)
            v6 = ph.emit("vector", lambda e: e.scalar_tensor_tensor(
                out=r[0:1, 1536:2048], in0=r[0:1, 512:1024], scalar=-1.0,
                in1=r[0:1, 1024:1536], op0=ALU.mult, op1=ALU.mult), sem=kb.sV)
            ph.wait("tensor", kb.sV, v6)
            mb1 = ph.emit("tensor", lambda e: e.matmul(
                kb.ps[2][:, 0:512], kb.onesr[:, :], r[0:1, 1024:1536],
                start=True, stop=True), sem=kb.sM)
            mb2 = ph.emit("tensor", lambda e: e.matmul(
                kb.ps[3][:, 0:512], kb.onesr[:, :], r[0:1, 1536:2048],
                start=True, stop=True), sem=kb.sM)
            ph.wait("vector", kb.sM, mb2)
            cb1 = ph.emit("vector", lambda e: e.tensor_copy(
                out=ar32[:, RB7:RB7 + 512], in_=kb.ps[2][:, 0:512]), sem=kb.sV)
            cb2 = ph.emit("vector", lambda e: e.tensor_copy(
                out=ar32[:, NB7:NB7 + 512], in_=kb.ps[3][:, 0:512]), sem=kb.sV)
            self._p7_rows_free = cb2
            ph.wait("vector", kb.sV, cb2)
            for kc in range(DC):
                sl = kc % 2
                ph.wait("vector", kb.sd(f"xst{sl}"), st_xn[sl])
                ph.emit("vector", lambda e, kc=kc, sl=sl: e.tensor_tensor(
                    out=ar32[:, SQ7 + sl * 512: SQ7 + (sl + 1) * 512],
                    in0=x7(kc), in1=ar32[:, RB7:RB7 + 512], op=ALU.mult))
                v_ap = ph.emit("vector", lambda e, kc=kc, sl=sl: e.tensor_tensor(
                    out=kb.ar[:, XST + sl * 512: XST + (sl + 1) * 512],
                    in0=ar32[:, SQ7 + sl * 512: SQ7 + (sl + 1) * 512],
                    in1=ar32[:, NB7:NB7 + 512], op=ALU.add), sem=kb.sV)
                ph.wait("sync", kb.sV, v_ap)
                st_xn[sl] = ph.dma("sync", kb.xn_s[tt, kc, :, :],
                                   kb.ar[:, XST + sl * 512: XST + (sl + 1) * 512],
                                   kb.sd(f"xst{sl}"))
            v_apply_prev = kb.sV.n
        for sl in range(2):
            ph.wait("sync", kb.sd(f"xst{sl}"), kb.sd(f"xst{sl}").n)
        ph.run()

    def tern_prep(self, ph, w_in, dstT, n_rc, thr_ap=None, thr_col=None, kcols=2048,
                  out_scale=1.0, col0=0):
        kb = self
        fb, wb = kb.fb, kb.wb
        npiece = kcols // 2048 if kcols % 2048 == 0 else None
        for rc in range(n_rc):
            sl = rc % 2
            ph.wait("sync", kb.sV, self._w8vfree.get(sl, 0))
            ph.wait("sync", kb.sA, self._w8afree.get(sl, 0))
            lw = ph.dma("sync", fb[:, sl * 2048: sl * 2048 + kcols],
                        w_in[rc * 128:(rc + 1) * 128, col0:col0 + kcols],
                        kb.sd(f"w8{sl}"))
            ph.wait("vector", kb.sd(f"w8{sl}"), lw)
            ph.wait("vector", kb.sA, self._sg8_free.get(sl, 0))
            v_ab = ph.emit("vector", lambda e, sl=sl: e.scalar_tensor_tensor(
                out=fb[:, 4096:4096 + kcols], in0=fb[:, sl * 2048: sl * 2048 + kcols],
                scalar=-1.0, in1=fb[:, sl * 2048: sl * 2048 + kcols],
                op0=ALU.mult, op1=ALU.max), sem=kb.sV)
            if thr_ap is None:
                ph.wait("vector", kb.sV, v_ab)
                v_s1 = ph.emit("vector", lambda e: e.reduce_sum(
                    out=kb.smf[:, 0:16],
                    in_=fb[:, 4096:6144].rearrange("p (g d) -> p g d", g=16),
                    axis=AX.X), sem=kb.sV)
                ph.wait("vector", kb.sV, v_s1)
                v_s2 = ph.emit("vector", lambda e: e.tensor_scalar(
                    out=kb.smf[:, 0:16], in0=kb.smf[:, 0:16],
                    scalar1=1.0 / 128, scalar2=1e-8, op0=ALU.mult,
                    op1=ALU.max), sem=kb.sV)
                ph.wait("vector", kb.sV, v_s2)
                v_s3 = ph.emit("vector", lambda e: e.tensor_scalar(
                    out=kb.smf[:, 16:32], in0=kb.smf[:, 0:16],
                    scalar1=0.5, scalar2=None, op0=ALU.mult), sem=kb.sV)
                ph.wait("vector", kb.sV, v_s3)
                if out_scale != 1.0:
                    v_s3 = ph.emit("vector", lambda e, osc=out_scale: e.tensor_scalar(
                        out=kb.smf[:, 0:16], in0=kb.smf[:, 0:16],
                        scalar1=osc, scalar2=None, op0=ALU.mult), sem=kb.sV)
                    ph.wait("vector", kb.sV, v_s3)
                v_m = ph.emit("vector", lambda e: e.tensor_tensor(
                    out=wb[:, 0:2048].rearrange("p (g d) -> p g d", g=16),
                    in0=fb[:, 4096:6144].rearrange("p (g d) -> p g d", g=16),
                    in1=kb.smf[:, 16:32].broadcast_to([128, 16, 128]),
                    op=ALU.is_gt), sem=kb.sV)
            else:
                ph.wait("vector", kb.sV, v_ab)
                v_m = ph.emit("vector", lambda e: e.tensor_scalar(
                    out=wb[:, 0:kcols], in0=fb[:, 4096:4096 + kcols],
                    scalar1=thr_ap, scalar2=None, op0=ALU.is_ge), sem=kb.sV)
            ph.wait("scalar", kb.sd(f"w8{sl}"), lw)
            ph.wait("scalar", kb.sV, self._q8_free)
            a_sg = ph.emit("scalar", lambda e, sl=sl: e.activation(
                out=wb[:, 2048:2048 + kcols], in_=fb[:, sl * 2048: sl * 2048 + kcols],
                func=AF.Sign), sem=kb.sA)
            self._sg8_free[sl] = a_sg
            self._w8vfree[sl] = v_ab
            self._w8afree[sl] = a_sg
            ph.wait("vector", kb.sA, a_sg)
            ph.wait("vector", kb.sd(f"tw8{1 - sl}"), self._tw8.get(1 - sl, 0))
            ph.wait("vector", kb.sd(f"tw8{sl}"), self._tw8.get(sl, 0))
            if thr_ap is None:
                v_q = ph.emit("vector", lambda e: e.tensor_tensor(
                    out=wb[:, 0:2048], in0=wb[:, 0:2048], in1=wb[:, 2048:4096],
                    op=ALU.mult), sem=kb.sV)
                ph.wait("vector", kb.sV, v_q)
                v_q = ph.emit("vector", lambda e: e.tensor_tensor(
                    out=wb[:, 4096:6144].rearrange("p (g d) -> p g d", g=16),
                    in0=wb[:, 0:2048].rearrange("p (g d) -> p g d", g=16),
                    in1=kb.smf[:, 0:16].broadcast_to([128, 16, 128]),
                    op=ALU.mult), sem=kb.sV)
            else:
                v_q = ph.emit("vector", lambda e: e.tensor_tensor(
                    out=wb[:, 4096:4096 + kcols], in0=wb[:, 0:kcols],
                    in1=wb[:, 2048:2048 + kcols], op=ALU.mult), sem=kb.sV)
            self._q8_free = v_q
            ph.wait("sync", kb.sV, v_q)
            ph.wait("sync", kb.sd(f"ts8{sl}"), self._st8.get(sl, 0))
            for j in range(kcols // 128):
                ph.dma("sync", wb[:, 8192 + sl * 2048 + j * 128: 8192 + sl * 2048 + (j + 1) * 128],
                       wb[:, 4096 + j * 128: 4096 + (j + 1) * 128],
                       kb.sd(f"tw8{sl}"), transpose=True)
            self._tw8[sl] = kb.sd(f"tw8{sl}").n
            ph.wait("scalar", kb.sd(f"tw8{sl}"), self._tw8[sl])
            self._st8[sl] = ph.dma("scalar", dstT[rc, :, col0:col0 + kcols],
                                   wb[:, 8192 + sl * 2048: 8192 + sl * 2048 + kcols],
                                   kb.sd(f"ts8{sl}"))

    def _p8_init(self):
        if not hasattr(self, "_w8vfree"):
            self._w8vfree = {}
            self._w8afree = {}
            self._sg8_free = {}
            self._q8_free = 0
            self._st8 = {}
            self._tw8 = {}

    def p8(self):
        kb = self
        self._p8_init()
        ph = Ph(kb, "p8a")
        self.tern_prep(ph, kb.wq_in, kb.wqT_s, 16, out_scale=128.0 ** -0.5)
        self.tern_prep(ph, kb.wk_in, kb.wkT_s, 16)
        self.tern_prep(ph, kb.wv_in, kb.wvT_s, 16)
        self.tern_prep(ph, kb.wo_in, kb.woT_s, 16)
        for sl in range(2):
            ph.wait("sync", kb.sd(f"ts8{sl}"), kb.sd(f"ts8{sl}").n)
        ph.run()
        self.p8b()
        self.p8c()

    def p8b(self):
        kb = self
        ph = Ph(kb, "p8b")
        ar, fb, wb = kb.ar, kb.fb, kb.wb
        T, TT = kb.T, kb.TT
        WT8, XN8 = 0, 32768
        mm_xn_prev = 0
        rp_users_prev = 0
        st_qk = {}
        ev_free = {}
        for pi, (src_s, dst_s) in enumerate([(kb.wqT_s, kb.qs_s), (kb.wkT_s, kb.ks_s)]):
            ph.wait("sync", kb.sM, mm_xn_prev)
            for oc in range(DC):
                ph.dma("sync", ar[:, WT8 + oc * 2048: WT8 + (oc + 1) * 2048],
                       src_s[oc, :, :], kb.sd("wt8"))
            ld_wt = kb.sd("wt8").n
            ph.wait("tensor", kb.sd("wt8"), ld_wt)
            for tt in range(TT):
                ph.wait("sync", kb.sM, mm_xn_prev)
                for kc in range(DC):
                    ph.dma("sync", ar[:, XN8 + kc * 512: XN8 + (kc + 1) * 512],
                           kb.xn_s[tt, kc, :, :], kb.sd("xn8"))
                ld_xn = kb.sd("xn8").n
                ph.wait("sync", kb.sV, rp_users_prev)
                l_cos = ph.dma("sync", fb[0:64, 0:512],
                               kb.c_rope[:, tt * 512:(tt + 1) * 512], kb.sd("rp8"))
                l_sin = ph.dma("sync", fb[0:64, 512:1024],
                               kb.c_rope[:, T + tt * 512: T + (tt + 1) * 512], kb.sd("rp8"))
                ph.wait("tensor", kb.sd("xn8"), ld_xn)
                for oc in range(DC):
                    b = oc % 4
                    ph.wait("tensor", kb.sV, ev_free.get(b, 0))
                    for kc in range(DC):
                        ph.emit("tensor", lambda e, oc=oc, kc=kc, b=b: e.matmul(
                            kb.ps[b][:],
                            ar[:, WT8 + oc * 2048 + kc * 128: WT8 + oc * 2048 + (kc + 1) * 128],
                            ar[:, XN8 + kc * 512: XN8 + (kc + 1) * 512],
                            start=(kc == 0), stop=(kc == DC - 1)), sem=kb.sM)
                    m_p = kb.sM.n
                    mm_xn_prev = m_p
                    sl = oc % 2
                    ph.wait("vector", kb.sM, m_p)
                    ph.wait("vector", kb.sd("rp8"), l_sin)
                    ph.wait("vector", kb.sd(f"qk8{sl}"), st_qk.get(sl, 0))
                    ph.emit("vector", lambda e, b=b: e.tensor_tensor(
                        out=fb[0:64, 1024:1536], in0=kb.ps[b][0:64, :],
                        in1=fb[0:64, 0:512], op=ALU.mult))
                    ph.emit("vector", lambda e, b=b: e.tensor_tensor(
                        out=fb[0:64, 1536:2048], in0=kb.ps[b][64:128, :],
                        in1=fb[0:64, 512:1024], op=ALU.mult))
                    ph.emit("vector", lambda e, sl=sl: e.tensor_tensor(
                        out=wb[:, 8192 + sl * 512: 8192 + sl * 512 + 512][0:64, :],
                        in0=fb[0:64, 1024:1536], in1=fb[0:64, 1536:2048],
                        op=ALU.subtract))
                    ph.emit("vector", lambda e, b=b: e.tensor_tensor(
                        out=fb[0:64, 1024:1536], in0=kb.ps[b][0:64, :],
                        in1=fb[0:64, 512:1024], op=ALU.mult))
                    ph.emit("vector", lambda e, b=b: e.tensor_tensor(
                        out=fb[0:64, 1536:2048], in0=kb.ps[b][64:128, :],
                        in1=fb[0:64, 0:512], op=ALU.mult))
                    v_hi = ph.emit("vector", lambda e, sl=sl: e.tensor_tensor(
                        out=wb[:, 8192 + sl * 512: 8192 + sl * 512 + 512][64:128, :],
                        in0=fb[0:64, 1024:1536], in1=fb[0:64, 1536:2048],
                        op=ALU.add), sem=kb.sV)
                    ev_free[b] = v_hi
                    ph.wait("sync", kb.sV, v_hi)
                    st_qk[sl] = ph.dma("sync", dst_s[oc, :, tt * 512:(tt + 1) * 512],
                                       wb[:, 8192 + sl * 512: 8192 + sl * 512 + 512],
                                       kb.sd(f"qk8{sl}"))
                rp_users_prev = kb.sV.n
        for sl in range(2):
            ph.wait("sync", kb.sd(f"qk8{sl}"), kb.sd(f"qk8{sl}").n)
        ph.run()

    def p8c(self):
        kb = self
        ph = Ph(kb, "p8c")
        ar, wb = kb.ar, kb.wb
        T, TT = kb.T, kb.TT
        WT8, XN8 = 0, 32768
        ph.wait("sync", kb.sM, kb.sM.n)
        for oc in range(DC):
            ph.dma("sync",
                   ar[:, WT8:WT8 + 32768].rearrange("p (kc o) -> p kc o", kc=16)
                   [:, :, oc * 128:(oc + 1) * 128],
                   kb.wvT_s[oc, :, :].rearrange("p (kc m) -> p kc m", kc=16),
                   kb.sd("wt8"))
        ld_wt = kb.sd("wt8").n
        ph.wait("tensor", kb.sd("wt8"), ld_wt)
        mm_xn_prev = 0
        ev_free = {}
        st_v = {}
        for tt in range(TT):
            ph.wait("sync", kb.sM, mm_xn_prev)
            for kc in range(DC):
                ph.dma("sync", ar[:, XN8 + kc * 512: XN8 + (kc + 1) * 512],
                       kb.xn_s[tt, kc, :, :], kb.sd("xn8"))
            ld_xn = kb.sd("xn8").n
            ph.wait("tensor", kb.sd("xn8"), ld_xn)
            for tc4 in range(4):
                tc = tt * 4 + tc4
                for dvg in range(4):
                    b = dvg
                    ph.wait("tensor", kb.sA, ev_free.get(b, 0))
                    for kc in range(DC):
                        ph.emit("tensor", lambda e, kc=kc, tc4=tc4, dvg=dvg, b=b: e.matmul(
                            kb.ps[b][:],
                            ar[:, XN8 + kc * 512 + tc4 * 128: XN8 + kc * 512 + (tc4 + 1) * 128],
                            ar[:, WT8 + kc * 2048 + dvg * 512: WT8 + kc * 2048 + (dvg + 1) * 512],
                            start=(kc == 0), stop=(kc == DC - 1)), sem=kb.sM)
                    m_v = kb.sM.n
                    mm_xn_prev = m_v
                    sl = dvg % 2
                    ph.wait("scalar", kb.sM, m_v)
                    ph.wait("scalar", kb.sd(f"v8{sl}"), st_v.get(sl, 0))
                    ev = ph.emit("scalar", lambda e, b=b, sl=sl: e.activation(
                        out=wb[:, 8192 + sl * 512: 8192 + (sl + 1) * 512],
                        in_=kb.ps[b][:], func=AF.Identity), sem=kb.sA)
                    ev_free[b] = ev
                    ph.wait("sync", kb.sA, ev)
                    st_v[sl] = ph.dma(
                        "sync",
                        kb.vs_s[dvg * 4:(dvg + 1) * 4, tc, :, :].rearrange("h p d -> p h d"),
                        wb[:, 8192 + sl * 512: 8192 + (sl + 1) * 512].rearrange(
                            "p (h d) -> p h d", h=4), kb.sd(f"v8{sl}"))
        for sl in range(2):
            ph.wait("sync", kb.sd(f"v8{sl}"), kb.sd(f"v8{sl}").n)
        ph.run()

    def p9(self):
        self._attn_init()
        self.attn("p9", causal=True, mask=True)


    def p10(self):
        kb = self
        self._p8_init()
        ph = Ph(kb, "p10a")
        fb = kb.fb
        specs = [(kb.w1_in, 44, 2048, [0], 0), (kb.w2_in, 44, 2048, [0], 1),
                 (kb.w3_in, 16, 2816, [0, 2816], 2)]
        g_free = [0, 0]
        for w_in, n_rc, wcols, col0s, idx in specs:
            parts = []
            pcount = 0
            for rc in range(n_rc):
                for c0 in col0s:
                    sl = pcount % 2
                    ph.wait("sync", kb.sV, g_free[sl])
                    lw = ph.dma("sync", fb[:, sl * 2848: sl * 2848 + wcols],
                                w_in[rc * 128:(rc + 1) * 128, c0:c0 + wcols],
                                kb.sd(f"gm{sl}"))
                    ph.wait("vector", kb.sd(f"gm{sl}"), lw)
                    col = 32 + (pcount % 44)
                    if pcount >= 44:
                        acc = True
                    if pcount < 44:
                        vv = ph.emit("vector", lambda e, sl=sl, col=col, wcols=wcols:
                                     e.reduce_sum(out=kb.smf[:, col:col + 1],
                                                  in_=fb[:, sl * 2848: sl * 2848 + wcols],
                                                  axis=AX.X, apply_absolute_value=True),
                                     sem=kb.sV)
                    else:
                        vv = ph.emit("vector", lambda e, sl=sl, wcols=wcols:
                                     e.reduce_sum(out=kb.smf[:, 80:81],
                                                  in_=fb[:, sl * 2848: sl * 2848 + wcols],
                                                  axis=AX.X, apply_absolute_value=True),
                                     sem=kb.sV)
                        ph.wait("vector", kb.sV, vv)
                        vv = ph.emit("vector", lambda e, col=col: e.tensor_tensor(
                            out=kb.smf[:, col:col + 1], in0=kb.smf[:, col:col + 1],
                            in1=kb.smf[:, 80:81], op=ALU.add), sem=kb.sV)
                    g_free[sl] = vv
                    pcount += 1
            nparts = min(pcount, 44)
            ph.wait("vector", kb.sV, kb.sV.n)
            v_t = ph.emit("vector", lambda e, nparts=nparts: e.reduce_sum(
                out=kb.smf[:, 81:82], in_=kb.smf[:, 32:32 + nparts], axis=AX.X),
                sem=kb.sV)
            ph.wait("tensor", kb.sV, v_t)
            m_t = ph.emit("tensor", lambda e: e.matmul(
                kb.ps[0][0:1, 0:1], kb.smf[:, 81:82], kb.onesc[:, 0:1],
                start=True, stop=True), sem=kb.sM)
            ph.wait("vector", kb.sM, m_t)
            nelem = n_rc * 128 * (wcols * len(col0s))
            v_g = ph.emit("vector", lambda e, nelem=nelem: e.tensor_scalar(
                out=kb.row[0:1, 0:1], in0=kb.ps[0][0:1, 0:1],
                scalar1=1.0 / nelem, scalar2=None, op0=ALU.mult), sem=kb.sV)
            ph.wait("vector", kb.sV, v_g)
            v_th = ph.emit("vector", lambda e: e.tensor_scalar(
                out=kb.row[0:1, 1:2], in0=kb.row[0:1, 0:1],
                scalar1=1e-8, scalar2=0.5, op0=ALU.add, op1=ALU.mult), sem=kb.sV)
            ph.wait("tensor", kb.sV, v_th)
            m_b = ph.emit("tensor", lambda e: e.matmul(
                kb.ps[1][:, 0:1], kb.onesr[:, :], kb.row[0:1, 1:2],
                start=True, stop=True), sem=kb.sM)
            ph.wait("vector", kb.sM, m_b)
            v_c = ph.emit("vector", lambda e, idx=idx: e.tensor_copy(
                out=kb.thr[:, idx:idx + 1], in_=kb.ps[1][:, 0:1]), sem=kb.sV)
            ph.wait("vector", kb.sV, v_c)
            v_z = ph.emit("vector", lambda e: e.memset(kb.smf[:, 32:76], 0.0), sem=kb.sV)
            g_free = [v_z, v_z]
        ph.run()
        ph2 = Ph(kb, "p10b")
        self.tern_prep(ph2, kb.w1_in, kb.w1T_s, 44, thr_ap=kb.thr[:, 0:1])
        self.tern_prep(ph2, kb.w2_in, kb.w2T_s, 44, thr_ap=kb.thr[:, 1:2])
        for c0, w in [(0, 2048), (2048, 2048), (4096, 1536)]:
            self.tern_prep(ph2, kb.w3_in, kb.w3T_s, 16, thr_ap=kb.thr[:, 2:3],
                           kcols=w, col0=c0)
        for sl in range(2):
            ph2.wait("sync", kb.sd(f"ts8{sl}"), kb.sd(f"ts8{sl}").n)
        ph2.run()
        self.p10c()

    def p10c(self):
        kb = self
        ph = Ph(kb, "p10c")
        ar, fb, wb = kb.ar, kb.fb, kb.wb
        T, TT = kb.T, kb.TT
        XN10, AO10, HID, W3SL, WOSL = 0, 8192, 16384, 38912, 50176
        wb32 = wb.bitcast(F32)

        def xn10(kc):
            return ar[:, XN10 + kc * 512: XN10 + (kc + 1) * 512]

        def ao10(kc):
            return ar[:, AO10 + kc * 512: AO10 + (kc + 1) * 512]

        def hid(hc):
            return ar[:, HID + hc * 512: HID + (hc + 1) * 512]

        w_free = [0, 0]
        wo_free = [0, 0]
        w3_free = [0, 0]
        sg_free = [0, 0]
        x1_bank_free = [0, 0]
        x2_bank_free = [0, 0]
        acc_bank_free = [0, 0]
        tp_bank_free = [0, 0]
        ost_free = [0] * 4
        x1sl_free = [0, 0]
        x2sl_free = [0, 0]
        mm_tile_prev = 0
        wl = 0

        for tt in range(TT):
            ph.wait("sync", kb.sM, mm_tile_prev)
            for kc in range(DC):
                ph.dma("sync", xn10(kc), kb.xn_s[tt, kc, :, :], kb.sd("xn0t"))
            ld_xn = kb.sd("xn0t").n
            for kc in range(DC):
                ph.dma("sync", ao10(kc), kb.ao_s[kc, :, tt * 512:(tt + 1) * 512],
                       kb.sd("ao0t"))
            ld_ao = kb.sd("ao0t").n
            ph.wait("tensor", kb.sd("xn0t"), ld_xn)
            ph.wait("tensor", kb.sd("ao0t"), ld_ao)
            for hc in range(44):
                bsl = hc % 2
                for which, wsrc in ((0, kb.w1T_s), (1, kb.w2T_s)):
                    sl = wl % 2
                    wl += 1
                    ph.wait("sync", kb.sM, w_free[sl])
                    lw = ph.dma("sync", wb[:, sl * 2048:(sl + 1) * 2048],
                                wsrc[hc, :, :], kb.sd(f"wa{sl}"))
                    ph.wait("tensor", kb.sd(f"wa{sl}"), lw)
                    bank = which * 2 + bsl
                    if which == 0:
                        ph.wait("tensor", kb.sA, x1_bank_free[bsl])
                    else:
                        ph.wait("tensor", kb.sV, x2_bank_free[bsl])
                    for kc in range(DC):
                        ph.emit("tensor", lambda e, kc=kc, sl=sl, bank=bank: e.matmul(
                            kb.ps[bank][:], wb[:, sl * 2048 + kc * 128: sl * 2048 + (kc + 1) * 128],
                            xn10(kc), start=(kc == 0), stop=(kc == DC - 1)), sem=kb.sM)
                    w_free[sl] = kb.sM.n
                    if which == 0:
                        m_x1 = kb.sM.n
                    else:
                        m_x2 = kb.sM.n
                ph.wait("scalar", kb.sM, m_x1)
                ph.wait("scalar", kb.sV, sg_free[bsl])
                a_sg = ph.emit("scalar", lambda e, bsl=bsl: e.activation(
                    out=fb[:, 1024 + bsl * 512: 1024 + (bsl + 1) * 512],
                    in_=kb.ps[bsl][:], func=AF.Sigmoid), sem=kb.sA)
                x1_bank_free[bsl] = a_sg
                ph.wait("vector", kb.sA, a_sg)
                ph.wait("vector", kb.sM, m_x2)
                v_h = ph.emit("vector", lambda e, hc=hc, bsl=bsl: e.tensor_tensor(
                    out=hid(hc), in0=fb[:, 1024 + bsl * 512: 1024 + (bsl + 1) * 512],
                    in1=kb.ps[2 + bsl][:], op=ALU.mult), sem=kb.sV)
                sg_free[bsl] = v_h
                x2_bank_free[bsl] = v_h
            v_hid_all = kb.sV.n
            ph.wait("tensor", kb.sV, v_hid_all)
            for oc in range(DC):
                osl = oc % 2
                ph.wait("sync", kb.sM, wo_free[osl])
                lwo = ph.dma("sync", ar[:, WOSL + osl * 2048: WOSL + (osl + 1) * 2048],
                             kb.woT_s[oc, :, :], kb.sd(f"wo{osl}"))
                ph.wait("sync", kb.sM, w3_free[osl])
                lw3 = ph.dma("sync", ar[:, W3SL + osl * 5632: W3SL + (osl + 1) * 5632],
                             kb.w3T_s[oc, :, :], kb.sd(f"w3{osl}"))
                bank = 4 + osl
                ph.wait("tensor", kb.sd(f"wo{osl}"), lwo)
                ph.wait("tensor", kb.sV, acc_bank_free[osl])
                for kc in range(DC):
                    ph.emit("tensor", lambda e, kc=kc, osl=osl, bank=bank: e.matmul(
                        kb.ps[bank][:],
                        ar[:, WOSL + osl * 2048 + kc * 128: WOSL + osl * 2048 + (kc + 1) * 128],
                        ao10(kc), start=(kc == 0), stop=False), sem=kb.sM)
                wo_free[osl] = kb.sM.n
                ph.wait("tensor", kb.sd(f"w3{osl}"), lw3)
                for hc in range(44):
                    ph.emit("tensor", lambda e, hc=hc, osl=osl, bank=bank: e.matmul(
                        kb.ps[bank][:],
                        ar[:, W3SL + osl * 5632 + hc * 128: W3SL + osl * 5632 + (hc + 1) * 128],
                        hid(hc), start=False, stop=(hc == 43)), sem=kb.sM)
                w3_free[osl] = kb.sM.n
                m_acc = kb.sM.n
                ph.wait("sync", kb.sV, x1sl_free[osl])
                lx1 = ph.dma("sync", fb[:, osl * 512:(osl + 1) * 512],
                             kb.xf_s[tt, oc, :, :], kb.sd(f"x10{osl}"))
                ph.wait("vector", kb.sM, m_acc)
                ph.wait("vector", kb.sd(f"x10{osl}"), lx1)
                ph.wait("vector", kb.sM, x2sl_free[osl])
                v_x2 = ph.emit("vector", lambda e, osl=osl, bank=bank: e.tensor_tensor(
                    out=fb[:, 2048 + osl * 512: 2048 + (osl + 1) * 512],
                    in0=kb.ps[bank][:], in1=fb[:, osl * 512:(osl + 1) * 512],
                    op=ALU.add), sem=kb.sV)
                x1sl_free[osl] = v_x2
                acc_bank_free[osl] = v_x2
                for tc in range(4):
                    tb = tc % 2
                    ph.wait("tensor", kb.sV, v_x2)
                    ph.wait("tensor", kb.sV, tp_bank_free[tb])
                    m_tp = ph.emit("tensor", lambda e, osl=osl, tc=tc, tb=tb: e.transpose(
                        kb.ps[6 + tb][:, 0:128],
                        fb[:, 2048 + osl * 512 + tc * 128: 2048 + osl * 512 + (tc + 1) * 128],
                        kb.idf[:]), sem=kb.sM)
                    ph.wait("vector", kb.sM, m_tp)
                    ph.wait("vector", kb.sd(f"yo{tc}"), ost_free[tc])
                    v_cp = ph.emit("vector", lambda e, tc=tc, tb=tb: e.tensor_copy(
                        out=wb32[:, 2048 + tc * 128: 2048 + (tc + 1) * 128],
                        in_=kb.ps[6 + tb][:, 0:128]), sem=kb.sV)
                    tp_bank_free[tb] = v_cp
                    ph.wait("scalar", kb.sV, v_cp)
                    ost_free[tc] = ph.dma(
                        "scalar",
                        kb.y_out[tt * 512 + tc * 128: tt * 512 + (tc + 1) * 128,
                                 oc * 128:(oc + 1) * 128],
                        wb32[:, 2048 + tc * 128: 2048 + (tc + 1) * 128],
                        kb.sd(f"yo{tc}"))
                x2sl_free[osl] = kb.sM.n
            mm_tile_prev = kb.sM.n
        for tc in range(4):
            ph.wait("scalar", kb.sd(f"yo{tc}"), kb.sd(f"yo{tc}").n)
        ph.run()



def host_consts(T):
    idb = np.eye(128).astype(BF)
    idf = np.eye(128, dtype=np.float32)
    onesc = np.ones((128, 2), np.float32)
    onesb = np.ones((128, 2)).astype(BF)
    onesr = np.ones((1, 128), np.float32)
    cc = np.zeros((128, NCC), np.float32)
    cc[:, CC_EPS_LN] = 1e-5
    cc[:, CC_3p75] = 3.75
    cc[:, CC_m8] = -8.0
    cc[:, CC_m10] = -10.0
    cc[:, CC_m4] = -4.0
    cc[:, CC_m6] = -6.0
    cc[:, CC_m2] = -2.0
    half = 64
    inv_freq = 1.0 / 10000.0 ** (np.arange(half, dtype=np.float32) / half)
    freqs = np.outer(inv_freq, np.arange(T, dtype=np.float32))
    rope = np.concatenate([np.cos(freqs), np.sin(freqs)], axis=1).astype(np.float32)
    p = np.arange(128)[:, None]
    f = np.arange(512)[None, :]
    cmask = np.stack([(f >= p + j * 128) for j in range(4)]).astype(BF)
    return dict(idb=idb, idf=idf, onesc=onesc, onesb=onesb, onesr=onesr,
                cc=cc, rope=rope, cmask=cmask)


def core_inputs(inputs, core, T):
    m = dict(
        x=np.ascontiguousarray(inputs["L0_attn_x"][core, :T]),
        W_attn=inputs["L0_attn_W_attn"],
        W_proj=inputs["L0_attn_W_proj"],
        lamvec=np.concatenate([
            inputs["L0_attn_lambda_q1"], inputs["L0_attn_lambda_k1"],
            inputs["L0_attn_lambda_q2"], inputs["L0_attn_lambda_k2"]]).reshape(1, -1),
        g_pk=inputs["L0_ffn_gate_W_packed"], g_sc=inputs["L0_ffn_gate_scales"],
        u_pk=inputs["L0_ffn_up_W_packed"], u_sc=inputs["L0_ffn_up_scales"],
        d_pk=inputs["L0_ffn_down_W_packed"], d_sc=inputs["L0_ffn_down_scales"],
        wq=inputs["L1_attn_w_q"], wk=inputs["L1_attn_w_k"],
        wv=inputs["L1_attn_w_v"], wo=inputs["L1_attn_w_o"],
        w1=inputs["L1_ffn_w1"], w2=inputs["L1_ffn_w2"], w3=inputs["L1_ffn_w3"],
    )
    m.update(host_consts(T))
    m = {k: np.ascontiguousarray(v) for k, v in m.items()}
    return m


def build(T, dump=(), phases=("p0", "p1")):
    nc = bass.Bass("TRN2", target_bir_lowering=False)
    kb = KB(nc, T, dump=dump)
    kb.declare()
    for p in phases:
        getattr(kb, p)()
    return nc, kb


ALL_PHASES = ("p0", "p1", "p2", "p3", "p5", "p6", "p7", "p8", "p9", "p10")


def kernel(**inputs):
    T = 4096
    nc, kb = build(T, phases=ALL_PHASES)
    in_maps = [core_inputs(inputs, c, T) for c in range(8)]
    res = run_bass_kernel_spmd(nc, in_maps, core_ids=list(range(8)))
    kb.ctx.close()
    return np.stack([res.results[c]["y"] for c in range(8)], axis=0)


# version 06 (final)
# speedup vs baseline: 1.2218x; 1.2218x over previous
import numpy as np
import ml_dtypes
from contextlib import ExitStack

import concourse.bass as bass
import concourse.mybir as mybir
from concourse.bass_utils import run_bass_kernel_spmd

F32 = mybir.dt.float32
BF16 = mybir.dt.bfloat16
I32 = mybir.dt.int32
AF = mybir.ActivationFunctionType
ALU = mybir.AluOpType
AX = mybir.AxisListType

D = 2048
DC = 16
NH = 16
BF = ml_dtypes.bfloat16

CC_EPS_LN = 0
CC_3p75 = 1
CC_m8 = 2
CC_m10 = 3
CC_m4 = 4
CC_m6 = 5
CC_m2 = 6
NCC = 7

ENGS = ("sync", "tensor", "vector", "scalar", "gpsimd")


class Sem:
    def __init__(self, kb, name):
        self.h = kb.ctx.enter_context(kb.nc.semaphore(name))
        self.n = 0


class Ph:

    def __init__(self, kb, name):
        self.kb = kb
        self.name = name
        self.q = {e: [] for e in ENGS}

    def wait(self, eng, sem, val):
        if val > 0:
            self.q[eng].append(lambda e, s=sem.h, v=val: e.wait_ge(s, v))

    def emit(self, eng, fn, sem=None, k=1):
        if sem is None:
            self.q[eng].append(fn)
            return None
        sem.n += k
        self.q[eng].append(lambda e, f=fn, s=sem.h, kk=k: f(e).then_inc(s, kk))
        return sem.n

    def dma(self, eng, out, in_, sem):
        return self.emit(eng, lambda e, o=out, i=in_: e.dma_start(out=o, in_=i),
                         sem=sem, k=16)

    def run(self):
        with self.kb.nc.Block(self.name) as blk:
            for eng in ENGS:
                qq = self.q[eng]
                if not qq and eng != "gpsimd":
                    continue

                def body(e, qq=qq):
                    for f in qq:
                        f(e)
                getattr(blk, eng)(body)


class KB:
    def __init__(self, nc, T, dump=()):
        self.nc = nc
        self.T = T
        self.TT = T // 512
        self.TC = T // 128
        self.dump = set(dump)
        self.ctx = ExitStack()

    def dram(self, name, shape, dtype, kind="Internal"):
        if name in self.dump:
            kind = "ExternalOutput"
        return self.nc.dram_tensor(name, shape, dtype, kind=kind).ap()

    def declare(self):
        E, T = self.ctx.enter_context, self.T
        self.x_in = self.dram("x", [T, D], F32, kind="ExternalInput")
        self.Wa = self.dram("W_attn", [D, 3 * D], F32, kind="ExternalInput")
        self.Wp = self.dram("W_proj", [D, D], F32, kind="ExternalInput")
        self.lamvec = self.dram("lamvec", [1, 4 * D], F32, kind="ExternalInput")
        self.g_pk = self.dram("g_pk", [8 * D, D // 2], I32, kind="ExternalInput")
        self.g_sc = self.dram("g_sc", [8 * D, D // 32], F32, kind="ExternalInput")
        self.u_pk = self.dram("u_pk", [8 * D, D // 2], I32, kind="ExternalInput")
        self.u_sc = self.dram("u_sc", [8 * D, D // 32], F32, kind="ExternalInput")
        self.d_pk = self.dram("d_pk", [D, 4 * D], I32, kind="ExternalInput")
        self.d_sc = self.dram("d_sc", [D, D // 4], F32, kind="ExternalInput")
        self.wq_in = self.dram("wq", [D, D], F32, kind="ExternalInput")
        self.wk_in = self.dram("wk", [D, D], F32, kind="ExternalInput")
        self.wv_in = self.dram("wv", [D, D], F32, kind="ExternalInput")
        self.wo_in = self.dram("wo", [D, D], F32, kind="ExternalInput")
        self.w1_in = self.dram("w1", [5632, D], F32, kind="ExternalInput")
        self.w2_in = self.dram("w2", [5632, D], F32, kind="ExternalInput")
        self.w3_in = self.dram("w3", [D, 5632], F32, kind="ExternalInput")
        self.c_idb = self.dram("idb", [128, 128], BF16, kind="ExternalInput")
        self.c_idf = self.dram("idf", [128, 128], F32, kind="ExternalInput")
        self.c_ones = self.dram("onesc", [128, 2], F32, kind="ExternalInput")
        self.c_onesb = self.dram("onesb", [128, 2], BF16, kind="ExternalInput")
        self.c_onesr = self.dram("onesr", [1, 128], F32, kind="ExternalInput")
        self.c_cc = self.dram("cc", [128, NCC], F32, kind="ExternalInput")
        self.c_rope = self.dram("rope", [64, 2 * T], F32, kind="ExternalInput")
        self.c_mask = self.dram("cmask", [4, 128, 512], BF16, kind="ExternalInput")
        self.y_out = self.dram("y", [T, D], F32, kind="ExternalOutput")
        self.xf_s = self.dram("xf_s", [self.TT, DC, 128, 512], F32)
        self.xn_s = self.dram("xn_s", [self.TT, DC, 128, 512], BF16)
        self.qs_s = self.dram("qs_s", [NH, 128, T], BF16)
        self.ks_s = self.dram("ks_s", [NH, 128, T], BF16)
        self.vs_s = self.dram("vs_s", [NH, self.TC, 128, 128], BF16)
        self.ao_s = self.dram("ao_s", [NH, 128, T], BF16)
        self.WgT_s = self.dram("WgT_s", [128, 128, D], BF16)
        self.WuT_s = self.dram("WuT_s", [128, 128, D], BF16)
        self.WdT2 = self.dram("WdT2", [4, 128, 128, 512], BF16)
        self.int_s = self.dram("int_s", [128, 128, T], BF16)
        self.wqT_s = self.dram("wqT_s", [16, 128, D], BF16)
        self.wkT_s = self.dram("wkT_s", [16, 128, D], BF16)
        self.wvT_s = self.dram("wvT_s", [16, 128, D], BF16)
        self.woT_s = self.dram("woT_s", [16, 128, D], BF16)
        self.w1T_s = self.dram("w1T_s", [44, 128, D], BF16)
        self.w2T_s = self.dram("w2T_s", [44, 128, D], BF16)
        self.w3T_s = self.dram("w3T_s", [16, 128, 5632], BF16)
        if "dbg1" in self.dump:
            self.dbg1 = self.dram("dbg1", [self.TC, 128, 24], F32)
        if "dbg5" in self.dump:
            self.dbg5 = self.dram("dbg5", [8, 128, 1024], F32, kind="ExternalOutput")
            self.dbg5b = self.dram("dbg5b", [128, 2048], BF16, kind="ExternalOutput")

        nc = self.nc
        self.ar = E(nc.sbuf_tensor("s_ar", [128, 65536], BF16))
        self.ar32 = self.ar.bitcast(F32)
        self.fb = E(nc.sbuf_tensor("s_fb", [128, 6144], F32))
        self.wb = E(nc.sbuf_tensor("s_wb", [128, 12288], BF16))
        self.ib = E(nc.sbuf_tensor("s_ib", [128, 2048], I32))
        self.smf = E(nc.sbuf_tensor("s_smf", [128, 96], F32))
        self.row = E(nc.sbuf_tensor("s_row", [1, 2048], F32))
        self.idb = E(nc.sbuf_tensor("s_idb", [128, 128], BF16))
        self.idf = E(nc.sbuf_tensor("s_idf", [128, 128], F32))
        self.onesc = E(nc.sbuf_tensor("s_onesc", [128, 2], F32))
        self.onesb = E(nc.sbuf_tensor("s_onesb", [128, 2], BF16))
        self.onesr = E(nc.sbuf_tensor("s_onesr", [1, 128], F32))
        self.cc = E(nc.sbuf_tensor("s_cc", [128, NCC], F32))
        self.q2s = E(nc.sbuf_tensor("s_q2s", [128, 1], F32))
        self.nub = E(nc.sbuf_tensor("s_nub", [128, 2], F32))
        self.thr = E(nc.sbuf_tensor("s_thr", [128, 4], F32))
        self.ps = [E(nc.psum_tensor(f"ps{i}", [128, 512], F32)) for i in range(8)]
        self.psb = [p.bitcast(BF16) for p in self.ps]

        self.msk = self.wb[:, 10240:12288]
        self.sL = Sem(self, "sem_L")
        self.sG = Sem(self, "sem_G")
        self.sM = Sem(self, "sem_M")
        self.sV = Sem(self, "sem_V")
        self.sA = Sem(self, "sem_A")
        self.sS = Sem(self, "sem_S")
        self.sP = Sem(self, "sem_P")
        self.sD = {}

    def sd(self, tag):
        if tag not in self.sD:
            self.sD[tag] = Sem(self, f"sd_{tag}")
        return self.sD[tag]

    def p0(self):
        kb = self
        ph = Ph(kb, "p0")
        for dst, src in [(kb.idb[:], kb.c_idb[:, :]), (kb.idf[:], kb.c_idf[:, :]),
                         (kb.onesc[:], kb.c_ones[:, :]), (kb.onesb[:], kb.c_onesb[:, :]),
                         (kb.onesr[:], kb.c_onesr[:, :]), (kb.cc[:], kb.c_cc[:, :]),
                         (kb.ar32[0:1, 0:4 * D], kb.lamvec[:, :])]:
            ph.dma("sync", dst, src, kb.sL)
        ld = kb.sL.n
        r = kb.ar32
        ph.wait("vector", kb.sL, ld)
        t0 = ph.emit("vector", lambda e: e.tensor_tensor(out=r[0:1, 4 * D:5 * D], in0=r[0:1, 0:D], in1=r[0:1, D:2 * D], op=ALU.mult), sem=kb.sV)
        ph.wait("vector", kb.sV, t0)
        t1 = ph.emit("vector", lambda e: e.reduce_sum(out=kb.smf[0:1, 64:65], in_=r[0:1, 4 * D:5 * D], axis=AX.X), sem=kb.sV)
        ph.wait("vector", kb.sV, t1)
        t2 = ph.emit("vector", lambda e: e.tensor_tensor(out=r[0:1, 4 * D:5 * D], in0=r[0:1, 2 * D:3 * D], in1=r[0:1, 3 * D:4 * D], op=ALU.mult), sem=kb.sV)
        ph.wait("vector", kb.sV, t2)
        v0 = ph.emit("vector", lambda e: e.reduce_sum(out=kb.smf[0:1, 65:66], in_=r[0:1, 4 * D:5 * D], axis=AX.X), sem=kb.sV)
        ph.wait("scalar", kb.sV, v0)
        a00 = ph.emit("scalar", lambda e: e.activation(out=kb.smf[0:1, 66:67], in_=kb.smf[0:1, 64:65], func=AF.Exp), sem=kb.sA)
        ph.wait("scalar", kb.sA, a00)
        a0 = ph.emit("scalar", lambda e: e.activation(out=kb.smf[0:1, 67:68], in_=kb.smf[0:1, 65:66], func=AF.Exp), sem=kb.sA)
        ph.wait("vector", kb.sA, a0)
        v05 = ph.emit("vector", lambda e: e.tensor_tensor(out=kb.smf[0:1, 68:69], in0=kb.smf[0:1, 66:67], in1=kb.smf[0:1, 67:68], op=ALU.subtract), sem=kb.sV)
        ph.wait("vector", kb.sV, v05)
        v1 = ph.emit("vector", lambda e: e.tensor_scalar(out=kb.smf[0:1, 68:69], in0=kb.smf[0:1, 68:69], scalar1=0.2, scalar2=-0.125, op0=ALU.add, op1=ALU.mult), sem=kb.sV)
        ph.wait("tensor", kb.sV, v1)
        ph.wait("tensor", kb.sL, ld)
        m0 = ph.emit("tensor", lambda e: e.matmul(kb.ps[0][:, 0:1], kb.onesr[:, :], kb.smf[0:1, 68:69], start=True, stop=True), sem=kb.sM)
        ph.wait("vector", kb.sM, m0)
        ph.emit("vector", lambda e: e.tensor_copy(out=kb.q2s[:], in_=kb.ps[0][:, 0:1]), sem=kb.sV)
        ph.run()

    def p1(self):
        kb = self
        ph = Ph(kb, "p1")
        TC = kb.TC
        ar32, fb, ar = kb.ar32, kb.fb, kb.ar
        XNST = 16384
        XNTK = 28672

        ld = [0] * TC
        a_sq = [0] * TC
        v_nb = [0] * TC
        a_xn = [0] * TC
        m_xf = [0] * TC
        m_xn = [0] * TC
        v_cp = {}
        st_tt = [0] * kb.TT
        tp_i = 0
        cp_order = []

        def xtok(sl):
            return fb[:, sl * 2048:(sl + 1) * 2048]

        def xntk(sl):
            return ar[:, XNTK + sl * 2048: XNTK + (sl + 1) * 2048]

        for tc in range(TC):
            sl = tc % 2
            tt = tc // 4
            if tc >= 2:
                ph.wait("sync", kb.sM, m_xf[tc - 2])
                ph.wait("sync", kb.sV, v_nb[tc - 2])
                ph.wait("sync", kb.sA, a_xn[tc - 2])
            sx = kb.sd(f"x{sl}")
            ld[tc] = ph.dma("sync", xtok(sl), kb.x_in[tc * 128:(tc + 1) * 128, :], sx)
            ph.wait("scalar", sx, ld[tc])
            if tc >= 2:
                ph.wait("scalar", kb.sM, m_xn[tc - 2])
                ph.wait("scalar", kb.sV, v_nb[tc - 2])
            c = sl
            ph.emit("scalar", lambda e, sl=sl, c=c: e.activation(
                out=xntk(sl), in_=xtok(sl), func=AF.Square,
                accum_out=kb.smf[:, 0 + c:1 + c]))
            a_sq[tc] = ph.emit("scalar", lambda e, c=c: e.copy(
                out=kb.smf[:, 0 + c:1 + c], in_=kb.smf[:, 0 + c:1 + c]), sem=kb.sA)
            ph.wait("vector", kb.sd(f"x{sl}"), ld[tc])
            ph.wait("vector", kb.sA, a_sq[tc])
            v_rs = ph.emit("vector", lambda e, sl=sl, c=c: e.reduce_sum(
                out=kb.smf[:, 2 + c:3 + c], in_=xtok(sl), axis=AX.X), sem=kb.sV)
            ph.wait("vector", kb.sV, v_rs)
            v_mean = ph.emit("vector", lambda e, c=c: e.tensor_scalar(
                out=kb.smf[:, 4 + c:5 + c], in0=kb.smf[:, 2 + c:3 + c],
                scalar1=1.0 / D, scalar2=None, op0=ALU.mult), sem=kb.sV)
            v_msq = ph.emit("vector", lambda e, c=c: e.tensor_scalar(
                out=kb.smf[:, 6 + c:7 + c], in0=kb.smf[:, 0 + c:1 + c],
                scalar1=1.0 / D, scalar2=None, op0=ALU.mult), sem=kb.sV)
            ph.wait("vector", kb.sV, v_mean)
            v_mm2 = ph.emit("vector", lambda e, c=c: e.tensor_tensor(
                out=kb.smf[:, 8 + c:9 + c], in0=kb.smf[:, 4 + c:5 + c],
                in1=kb.smf[:, 4 + c:5 + c], op=ALU.mult), sem=kb.sV)
            ph.wait("vector", kb.sV, v_mm2)
            v_var = ph.emit("vector", lambda e, c=c: e.tensor_tensor(
                out=kb.smf[:, 10 + c:11 + c], in0=kb.smf[:, 6 + c:7 + c],
                in1=kb.smf[:, 8 + c:9 + c], op=ALU.subtract), sem=kb.sV)
            ph.wait("scalar", kb.sV, v_var)
            a_sd = ph.emit("scalar", lambda e, c=c: e.activation(
                out=kb.smf[:, 12 + c:13 + c], in_=kb.smf[:, 10 + c:11 + c],
                func=AF.Sqrt, bias=kb.cc[:, CC_EPS_LN:CC_EPS_LN + 1]), sem=kb.sA)
            ph.wait("vector", kb.sA, a_sd)
            v_rec = ph.emit("vector", lambda e, c=c: e.reciprocal(
                out=kb.smf[:, 14 + c:15 + c], in_=kb.smf[:, 12 + c:13 + c]), sem=kb.sV)
            ph.wait("vector", kb.sV, v_rec)
            v_nb[tc] = ph.emit("vector", lambda e, c=c: e.scalar_tensor_tensor(
                out=kb.smf[:, 16 + c:17 + c], in0=kb.smf[:, 4 + c:5 + c],
                scalar=-1.0, in1=kb.smf[:, 14 + c:15 + c],
                op0=ALU.mult, op1=ALU.mult), sem=kb.sV)
            ph.wait("scalar", kb.sV, v_nb[tc])
            a_xn[tc] = ph.emit("scalar", lambda e, sl=sl, c=c: e.activation(
                out=xntk(sl), in_=xtok(sl), func=AF.Identity,
                scale=kb.smf[:, 14 + c:15 + c],
                bias=kb.smf[:, 16 + c:17 + c]), sem=kb.sA)
            if "dbg1" in kb.dump:
                ph.wait("sync", kb.sV, v_nb[tc])
                ph.dma("sync", kb.dbg1[tc, :, :], kb.smf[:, 0:24], kb.sS)
            for phase_sel in (0, 1):
                if phase_sel == 0:
                    ph.wait("tensor", kb.sd(f"x{sl}"), ld[tc])
                else:
                    ph.wait("tensor", kb.sA, a_xn[tc])
                for kc in range(DC):
                    b = tp_i % 4
                    if tp_i >= 4:
                        ph.wait("tensor", kb.sV, cp_order[tp_i - 4])
                    if phase_sel == 0:
                        m = ph.emit("tensor", lambda e, sl=sl, kc=kc, b=b: e.transpose(
                            kb.ps[b][:, 0:128], xtok(sl)[:, kc * 128:(kc + 1) * 128],
                            kb.idf[:]), sem=kb.sM)
                    else:
                        m = ph.emit("tensor", lambda e, sl=sl, kc=kc, b=b: e.transpose(
                            kb.psb[b][:, 0:128], xntk(sl)[:, kc * 128:(kc + 1) * 128],
                            kb.idb[:]), sem=kb.sM)
                    ph.wait("vector", kb.sM, m)
                    if tt > 0 and tp_i % (32 * 4) < 2:
                        ph.wait("vector", kb.sS, st_tt[tt - 1])
                    col = (tc % 4) * 128
                    if phase_sel == 0:
                        cv = ph.emit("vector", lambda e, kc=kc, b=b, col=col: e.tensor_copy(
                            out=ar32[:, kc * 512 + col: kc * 512 + col + 128],
                            in_=kb.ps[b][:, 0:128]), sem=kb.sV)
                    else:
                        cv = ph.emit("vector", lambda e, kc=kc, b=b, col=col: e.tensor_copy(
                            out=ar[:, XNST + kc * 512 + col: XNST + kc * 512 + col + 128],
                            in_=kb.psb[b][:, 0:128]), sem=kb.sV)
                    cp_order.append(cv)
                    tp_i += 1
                if phase_sel == 0:
                    m_xf[tc] = kb.sM.n
                else:
                    m_xn[tc] = kb.sM.n
            if tc % 4 == 3:
                ph.wait("sync", kb.sV, kb.sV.n)
                for kc in range(DC):
                    ph.dma("sync", kb.xf_s[tt, kc, :, :],
                           ar32[:, kc * 512:(kc + 1) * 512], kb.sS)
                    ph.dma("sync", kb.xn_s[tt, kc, :, :],
                           ar[:, XNST + kc * 512: XNST + (kc + 1) * 512], kb.sS)
                st_tt[tt] = kb.sS.n
        ph.wait("sync", kb.sS, kb.sS.n)
        ph.run()


    def p2(self):
        kb = self
        ph = Ph(kb, "p2")
        T, TT, TC = kb.T, kb.TT, kb.TC
        ar, wb = kb.ar, kb.wb

        def xnr(kc):
            return ar[:, kc * T:(kc + 1) * T]

        for kc in range(DC):
            for tt in range(TT):
                ph.dma("sync", ar[:, kc * T + tt * 512: kc * T + (tt + 1) * 512],
                       kb.xn_s[tt, kc, :, :], kb.sL)
        ld_xn = kb.sL.n

        def wt(kc):
            return wb[:, kc * 512:(kc + 1) * 512]

        def stg(s):
            return wb[:, 8192 + s * 512: 8192 + (s + 1) * 512]

        ph.wait("tensor", kb.sL, ld_xn)

        gld = {}
        m_kc = {}
        evac_b = {}
        st_s = [0] * 4
        gi = 0
        si = 0

        for tp in range(TT // 2):
            for g in range(8):
                for kc in range(DC):
                    if gi > 0:
                        ph.wait("gpsimd", kb.sM, m_kc[(gi - 1, kc)])
                    gld[(gi, kc)] = ph.dma(
                        "gpsimd", wt(kc),
                        kb.Wa[kc * 128:(kc + 1) * 128, g * 512:(g + 1) * 512],
                        kb.sd(f"wt{kc}"))
                for kc in range(DC):
                    ph.wait("tensor", kb.sd(f"wt{kc}"), gld[(gi, kc)])
                    for j in range(4):
                        for h2 in range(2):
                            b = j * 2 + h2
                            if kc == 0 and gi > 0:
                                ph.wait("tensor", kb.sA, evac_b.get((gi - 1, b), 0))
                            ph.emit("tensor", lambda e, kc=kc, j=j, h2=h2, b=b, tp=tp: e.matmul(
                                kb.ps[b][:],
                                wt(kc)[:, j * 128:(j + 1) * 128],
                                xnr(kc)[:, tp * 1024 + h2 * 512: tp * 1024 + (h2 + 1) * 512],
                                start=(kc == 0), stop=(kc == DC - 1)),
                                sem=kb.sM)
                    m_kc[(gi, kc)] = kb.sM.n
                for j in range(4):
                    nc_ = g * 4 + j
                    for h2 in range(2):
                        b = j * 2 + h2
                        s = si % 4
                        si += 1
                        ph.wait("scalar", kb.sM, m_kc[(gi, DC - 1)])
                        ph.wait("scalar", kb.sd(f"st{s}"), st_s[s])
                        if nc_ < 8:
                            ev = ph.emit("scalar", lambda e, b=b, s=s: e.activation(
                                out=stg(s), in_=kb.ps[b][:], func=AF.Identity,
                                scale=0.125), sem=kb.sA)
                        elif nc_ < 16:
                            ev = ph.emit("scalar", lambda e, b=b, s=s: e.activation(
                                out=stg(s), in_=kb.ps[b][:], func=AF.Identity), sem=kb.sA)
                        elif nc_ < 24:
                            ev = ph.emit("scalar", lambda e, b=b, s=s: e.activation(
                                out=stg(s), in_=kb.ps[b][:], func=AF.Identity,
                                scale=kb.q2s[:]), sem=kb.sA)
                        else:
                            ev = ph.emit("scalar", lambda e, b=b, s=s: e.activation(
                                out=stg(s), in_=kb.ps[b][:], func=AF.Identity), sem=kb.sA)
                        evac_b[(gi, b)] = ev
                        t0 = tp * 1024 + h2 * 512
                        c = nc_ % 8
                        h0, h1 = 2 * c, 2 * c + 1
                        if nc_ < 8:
                            dsts = [(kb.qs_s[h0, 0:64, t0:t0 + 512], 0),
                                    (kb.qs_s[h1, 0:64, t0:t0 + 512], 64)]
                        elif nc_ < 16:
                            dsts = [(kb.ks_s[h0, 0:64, t0:t0 + 512], 0),
                                    (kb.ks_s[h1, 0:64, t0:t0 + 512], 64)]
                        elif nc_ < 24:
                            dsts = [(kb.qs_s[h0, 64:128, t0:t0 + 512], 0),
                                    (kb.qs_s[h1, 64:128, t0:t0 + 512], 64)]
                        else:
                            dsts = [(kb.ks_s[h0, 64:128, t0:t0 + 512], 0),
                                    (kb.ks_s[h1, 64:128, t0:t0 + 512], 64)]
                        ph.wait("sync", kb.sA, ev)
                        for dst, r0 in dsts:
                            ph.dma("sync", dst, stg(s)[r0:r0 + 64, :], kb.sd(f"st{s}"))
                        st_s[s] = kb.sd(f"st{s}").n
                gi += 1

        for tcg in range(TC // 8):
            for dvg in range(4):
                for kc in range(DC):
                    ph.wait("gpsimd", kb.sM, m_kc[(gi - 1, kc)])
                    gld[(gi, kc)] = ph.dma(
                        "gpsimd", wt(kc),
                        kb.Wa[kc * 128:(kc + 1) * 128,
                              2 * D + dvg * 512: 2 * D + (dvg + 1) * 512],
                        kb.sd(f"wt{kc}"))
                for kc in range(DC):
                    ph.wait("tensor", kb.sd(f"wt{kc}"), gld[(gi, kc)])
                    for b in range(8):
                        tc = tcg * 8 + b
                        if kc == 0:
                            ph.wait("tensor", kb.sA, evac_b.get((gi - 1, b), 0))
                        ph.emit("tensor", lambda e, kc=kc, b=b, tc=tc: e.matmul(
                            kb.ps[b][:],
                            xnr(kc)[:, tc * 128:(tc + 1) * 128],
                            wt(kc)[:],
                            start=(kc == 0), stop=(kc == DC - 1)),
                            sem=kb.sM)
                    m_kc[(gi, kc)] = kb.sM.n
                for b in range(8):
                    tc = tcg * 8 + b
                    s = si % 4
                    si += 1
                    ph.wait("scalar", kb.sM, m_kc[(gi, DC - 1)])
                    ph.wait("scalar", kb.sd(f"st{s}"), st_s[s])
                    ev = ph.emit("scalar", lambda e, b=b, s=s: e.activation(
                        out=stg(s), in_=kb.ps[b][:], func=AF.Identity,
                        scale=0.8), sem=kb.sA)
                    evac_b[(gi, b)] = ev
                    ph.wait("sync", kb.sA, ev)
                    ph.dma("sync",
                           kb.vs_s[dvg * 4:(dvg + 1) * 4, tc, :, :].rearrange("h p d -> p h d"),
                           stg(s)[:].rearrange("p (h d) -> p h d", h=4), kb.sd(f"st{s}"))
                    st_s[s] = kb.sd(f"st{s}").n
                gi += 1
        for s in range(4):
            ph.wait("sync", kb.sd(f"st{s}"), kb.sd(f"st{s}").n)
        ph.run()

    def attn(self, name, causal, mask=False, ph=None):
        g = self.attn_gen(name, causal, mask, ph)
        for _ in g:
            pass

    def attn_gen(self, name, causal, mask=False, ext_ph=None):
        kb = self
        ph = ext_ph if ext_ph is not None else Ph(kb, name)
        T, TT, TC = kb.T, kb.TT, kb.TC
        ar, fb = kb.ar, kb.fb

        def qv(hp):
            return ar[:, 0:T]

        def kv(hp):
            return ar[:, 4096:4096 + T]

        def vv(hp):
            return ar[:, 8192:8192 + T]

        def trash(hp):
            return ar[:, 12288:12288 + T]

        def pT(p):
            return ar[:, 16384 + p * 512: 16384 + (p + 1) * 512]

        def aost(p):
            return ar[:, 20480 + p * 512: 20480 + (p + 1) * 512]

        def rb(p):
            return kb.ar32[:, 10752 + p * 512: 10752 + (p + 1) * 512]

        NS = 2
        NPT = 8
        BK_OUT = 2
        BK_DEN = 4
        BK_BC = 5
        ld_head = [0] * NH
        done_head = [0] * NH
        ao_st = [0, 0]
        pt_mm = {}
        exp_of_bank = {}
        den_free = getattr(self, "_den_free", [0, 0])
        out_free = getattr(self, "_out_free", [0, 0])
        gtk = getattr(self, "_attn_gtk", 0)

        if mask:
            for j in range(4):
                ph.dma("sync", kb.msk[:, j * 512:(j + 1) * 512],
                       kb.c_mask[j, :, :], kb.sd("msk"))
            self._msk_ld = kb.sd("msk").n

        def issue_loads(h):
            if h >= 1:
                ph.wait("sync", kb.sM, done_head[h - 1])
            lq = ph.dma("sync", qv(0), kb.qs_s[h, :, :], kb.sd("aq"))
            lk = ph.dma("sync", kv(0), kb.ks_s[h, :, :], kb.sd("ak"))
            lv = ph.dma(
                "sync", vv(0).rearrange("p (tc d) -> p tc d", d=128),
                kb.vs_s[h, :, :, :].rearrange("tc p d -> p tc d"), kb.sd("av"))
            ld_head[h] = (lq, lk, lv)

        for h in range(NH):
            hp = 0
            issue_loads(h)
            for which, srcq in (("q", qv(hp)), ("k", kv(hp))):
                if h >= 1:
                    ph.wait("scalar", kb.sM, done_head[h - 1])
                ph.wait("scalar", kb.sd("aq"), ld_head[h][0])
                ph.wait("scalar", kb.sd("ak"), ld_head[h][1])
                ph.wait("scalar", kb.sM, self._trash_free)
                a = ph.emit("scalar", lambda e, srcq=srcq: e.activation(
                    out=trash(hp), in_=srcq, func=AF.Square), sem=kb.sA)
                base = 72 if which == "q" else 80
                last_rd = max(self._ps7_free, self._ps7_bcast_free)
                for i in range(TT):
                    ph.wait("tensor", kb.sA, a)
                    if last_rd:
                        ph.wait("tensor", kb.sV, last_rd)
                    mmv = ph.emit("tensor", lambda e, i=i: e.matmul(
                        kb.ps[BK_BC][0:1, 0:512], kb.onesb[:, 0:1],
                        trash(hp)[:, i * 512:(i + 1) * 512],
                        start=True, stop=True), sem=kb.sM)
                    ph.wait("vector", kb.sM, mmv)
                    last_rd = ph.emit("vector", lambda e, i=i, base=base: e.reduce_max(
                        out=kb.smf[0:1, base + i: base + i + 1],
                        in_=kb.ps[BK_BC][0:1, 0:512], axis=AX.X), sem=kb.sV)
                self._trash_free = kb.sM.n
                self._ps7_free = last_rd
                ph.wait("vector", kb.sV, last_rd)
                dst = 88 if which == "q" else 89
                ph.emit("vector", lambda e, base=base, dst=dst: e.reduce_max(
                    out=kb.smf[0:1, dst:dst + 1], in_=kb.smf[0:1, base:base + TT],
                    axis=AX.X), sem=kb.sV)
            v_mx = kb.sV.n
            ph.wait("vector", kb.sV, v_mx)
            v_t = ph.emit("vector", lambda e: e.tensor_tensor(
                out=kb.smf[0:1, 90:91], in0=kb.smf[0:1, 88:89],
                in1=kb.smf[0:1, 89:90], op=ALU.mult), sem=kb.sV)
            ph.wait("scalar", kb.sV, v_t)
            a_sq2 = ph.emit("scalar", lambda e: e.activation(
                out=kb.smf[0:1, 91:92], in_=kb.smf[0:1, 90:91], func=AF.Sqrt), sem=kb.sA)
            ph.wait("vector", kb.sA, a_sq2)
            v_ng = ph.emit("vector", lambda e: e.tensor_scalar(
                out=kb.smf[0:1, 92:93], in0=kb.smf[0:1, 91:92],
                scalar1=-1.02, scalar2=-0.25, op0=ALU.mult, op1=ALU.add), sem=kb.sV)
            ph.wait("tensor", kb.sV, v_ng)
            ph.wait("tensor", kb.sV, self._ps7_bcast_free)
            m_bc = ph.emit("tensor", lambda e: e.matmul(
                kb.ps[BK_BC][:, 0:1], kb.onesr[:, :], kb.smf[0:1, 92:93],
                start=True, stop=True), sem=kb.sM)
            ph.wait("vector", kb.sM, m_bc)
            ph.wait("vector", kb.sA, self._nub_users[hp])
            v_nub = ph.emit("vector", lambda e: e.tensor_copy(
                out=kb.nub[:, 0:1], in_=kb.ps[BK_BC][:, 0:1]), sem=kb.sV)
            self._ps7_free = v_nub
            ph.wait("scalar", kb.sV, v_nub)

            ph.wait("tensor", kb.sd("aq"), ld_head[h][0])
            ph.wait("tensor", kb.sd("ak"), ld_head[h][1])
            ph.wait("tensor", kb.sd("av"), ld_head[h][2])
            for tqt in range(TT):
                par = tqt % 2
                ob = BK_OUT + par
                db = BK_DEN
                n_tk = 4 * (tqt + 1) if causal else TC
                first_tk = True
                LOOK = 2
                pending = []

                def emit_S(tkc, gi):
                    b = gi % NS
                    p = gi % NPT
                    ph.wait("tensor", kb.sA, exp_of_bank.get(b, 0))
                    m_s = ph.emit("tensor", lambda e, b=b, tkc=tkc, tqt=tqt: e.matmul(
                        kb.ps[b][:], kv(hp)[:, tkc * 128:(tkc + 1) * 128],
                        qv(hp)[:, tqt * 512:(tqt + 1) * 512],
                        start=True, stop=True), sem=kb.sM)
                    ph.wait("scalar", kb.sM, max(m_s, pt_mm.get(p, 0)))
                    a_exp = ph.emit("scalar", lambda e, b=b, p=p: e.activation(
                        out=pT(p), in_=kb.ps[b][:], func=AF.Exp,
                        bias=kb.nub[:, 0:1]), sem=kb.sA)
                    exp_of_bank[b] = a_exp
                    pt_sem, pt_ready = kb.sA, a_exp
                    if causal and mask and tkc >= 4 * tqt:
                        j = tkc - 4 * tqt
                        ph.wait("vector", kb.sd("msk"), self._msk_ld)
                        ph.wait("vector", kb.sA, a_exp)
                        v_m = ph.emit("vector", lambda e, p=p, j=j: e.tensor_tensor(
                            out=pT(p), in0=pT(p),
                            in1=kb.msk[:, j * 512:(j + 1) * 512], op=ALU.mult),
                            sem=kb.sV)
                        pt_sem, pt_ready = kb.sV, v_m
                    pending.append((tkc, p, pt_sem, pt_ready))

                def emit_DV(entry):
                    nonlocal first_tk
                    tkc, p, pt_sem, pt_ready = entry
                    ph.wait("tensor", pt_sem, pt_ready)
                    if first_tk:
                        ph.wait("tensor", kb.sV, den_free[0])
                    ph.emit("tensor", lambda e, p=p, st=first_tk,
                            sp=(tkc == n_tk - 1): e.matmul(
                        kb.ps[db][0:1, 0:512], kb.onesb[:, 0:1], pT(p),
                        start=st, stop=sp), sem=kb.sM)
                    if first_tk:
                        ph.wait("tensor", kb.sV, out_free[par])
                    m_v = ph.emit("tensor", lambda e, p=p, tkc=tkc, ob=ob,
                                  st=first_tk, sp=(tkc == n_tk - 1): e.matmul(
                        kb.ps[ob][:], vv(hp)[:, tkc * 128:(tkc + 1) * 128], pT(p),
                        start=st, stop=sp), sem=kb.sM)
                    pt_mm[p] = m_v
                    first_tk = False

                for tkc in range(n_tk):
                    emit_S(tkc, gtk)
                    gtk += 1
                    if len(pending) > LOOK:
                        emit_DV(pending.pop(0))
                    if tkc % 8 == 7:
                        yield
                while pending:
                    emit_DV(pending.pop(0))
                m_all = kb.sM.n
                ph.wait("vector", kb.sM, m_all)
                v_rc = ph.emit("vector", lambda e, par=par: e.reciprocal(
                    out=kb.row[0:1, par * 512:(par + 1) * 512],
                    in_=kb.ps[db][0:1, 0:512]), sem=kb.sV)
                den_free[0] = v_rc
                ph.wait("tensor", kb.sV, v_rc)
                ph.wait("tensor", kb.sV, self._ps7_bcast_free)
                m_b2 = ph.emit("tensor", lambda e, par=par: e.matmul(
                    kb.ps[BK_BC][:, 0:512], kb.onesr[:, :],
                    kb.row[0:1, par * 512:(par + 1) * 512],
                    start=True, stop=True), sem=kb.sM)
                ph.wait("vector", kb.sM, m_b2)
                v_cb = ph.emit("vector", lambda e, par=par: e.tensor_copy(
                    out=rb(par), in_=kb.ps[BK_BC][:, 0:512]), sem=kb.sV)
                self._ps7_bcast_free = v_cb
                sp_ = par
                ph.wait("vector", kb.sd(f"ao{sp_}"), ao_st[sp_])
                v_o = ph.emit("vector", lambda e, sp_=sp_, ob=ob: e.scalar_tensor_tensor(
                    out=aost(sp_), in0=kb.ps[ob][:], scalar=1.0, in1=rb(sp_),
                    op0=ALU.mult, op1=ALU.mult), sem=kb.sV)
                out_free[par] = v_o
                ph.wait("gpsimd", kb.sV, v_o)
                ph.dma("gpsimd", kb.ao_s[h, :, tqt * 512:(tqt + 1) * 512],
                       aost(sp_), kb.sd(f"ao{sp_}"))
                ao_st[sp_] = kb.sd(f"ao{sp_}").n
                yield
            done_head[h] = kb.sM.n
            self._nub_users[hp] = kb.sA.n
            yield
        for sp_ in range(2):
            ph.wait("gpsimd", kb.sd(f"ao{sp_}"), kb.sd(f"ao{sp_}").n)
        self._den_free = den_free
        self._out_free = out_free
        self._attn_gtk = gtk
        if ext_ph is None:
            ph.run()

    def _attn_init(self):
        if not hasattr(self, "_trash_free"):
            self._trash_free = 0
            self._ps7_free = 0
            self._ps7_bcast_free = 0
            self._nub_users = [0, 0]

    def p3(self):
        self._attn_init()
        self.attn("p3", causal=False)

    def p35(self):
        self._attn_init()
        ph = Ph(self, "p35")
        ga = self.attn_gen("p35", causal=False, mask=False, ext_ph=ph)
        gp = self.p5_gen(ph)
        done_a = done_p = False
        while not (done_a and done_p):
            if not done_a:
                try:
                    next(ga)
                except StopIteration:
                    done_a = True
            if not done_p:
                try:
                    next(gp)
                except StopIteration:
                    done_p = True
        ph.run()


    def p5(self):
        ph = Ph(self, "p5")
        for _ in self.p5_gen(ph):
            pass
        ph.run()

    def p5_gen(self, ph):
        kb = self
        fb, wb, ib = kb.fb, kb.wb, kb.ib
        ar32 = kb.ar32
        ari = kb.ar.bitcast(I32)
        F_SC = 0

        def tS(k):
            q = k % 3
            return ar32[:, 12288 + q * 1024: 12288 + (q + 1) * 1024]

        def tR1(k):
            q = k % 2
            return ar32[:, 15360 + q * 1024: 15360 + (q + 1) * 1024]

        def tR2(k):
            q = k % 2
            return ar32[:, 17408 + q * 1024: 17408 + (q + 1) * 1024]

        def tM0(k):
            q = k % 2
            return ar32[:, 19456 + q * 1024: 19456 + (q + 1) * 1024]

        def tMG(k):
            q = k % 2
            return ar32[:, 21504 + q * 1024: 21504 + (q + 1) * 1024]

        def nibN(k):
            q = k % 2
            return ari[:, 23552 + q * 1024: 23552 + (q + 1) * 1024]

        def nibM(k):
            q = k % 2
            return ari[:, 25600 + q * 1024: 25600 + (q + 1) * 1024]

        def vbf(vp):
            return wb[:, 1024 + vp * 2048: 1024 + (vp + 1) * 2048]

        def ost(vp):
            return wb[:, 5120 + vp * 2048: 5120 + (vp + 1) * 2048]

        pieces = []
        for rc in range(128):
            pieces.append((kb.g_pk[rc * 128:(rc + 1) * 128, :],
                           kb.g_sc[rc * 128:(rc + 1) * 128, :],
                           kb.WgT_s[rc, :, :]))
        for rc in range(128):
            pieces.append((kb.u_pk[rc * 128:(rc + 1) * 128, :],
                           kb.u_sc[rc * 128:(rc + 1) * 128, :],
                           kb.WuT_s[rc, :, :]))
        for rc in range(16):
            for pc in range(8):
                dst4 = kb.WdT2[rc // 4, pc * 16:(pc + 1) * 16, :,
                               (rc % 4) * 128:(rc % 4) * 128 + 128]
                pieces.append((kb.d_pk[rc * 128:(rc + 1) * 128, pc * 1024:(pc + 1) * 1024],
                               kb.d_sc[rc * 128:(rc + 1) * 128, pc * 64:(pc + 1) * 64],
                               ("wd", dst4)))
        NP = len(pieces)
        NK = 2 * NP

        ld_pk = {}
        ld_sc = {}
        vm = {}
        vr2 = {}
        vmag = {}
        vs_ = {}
        pval = {}
        pout = {}
        p_m0_hist = {}
        m_tp = {}
        st_p = {}
        tp_hist = []

        def issue_piece_loads(pi):
            if pi >= NP:
                return
            pp = pi % 2
            pk, sc, dst = pieces[pi]
            if pi >= 2:
                ph.wait("sync", kb.sV, vm.get(2 * (pi - 2) + 1, 0))
                ph.wait("sync", kb.sV, pout.get(2 * (pi - 2) + 1, 0))
            ld_pk[pi] = ph.dma("sync", ib[:, pp * 1024:(pp + 1) * 1024], pk,
                               kb.sd(f"pk{pp}"))
            ld_sc[pi] = ph.dma("sync", fb[:, F_SC + pp * 64: F_SC + (pp + 1) * 64],
                               sc, kb.sd(f"sc{pp}"))

        def stageDA(k):
            pi, pl = k // 2, k % 2
            pp = pi % 2
            ph.wait("vector", kb.sd(f"pk{pp}"), ld_pk[pi])
            ph.wait("vector", kb.sA, vr2.get(k - 2, 0))
            if pl == 0:
                ph.emit("vector", lambda e, pp=pp, k=k: e.tensor_scalar(
                    out=nibN(k), in0=ib[:, pp * 1024:(pp + 1) * 1024],
                    scalar1=15, scalar2=None, op0=ALU.bitwise_and), sem=kb.sV)
                vm[k] = ph.emit("vector", lambda e, pp=pp, k=k: e.tensor_scalar(
                    out=nibM(k), in0=ib[:, pp * 1024:(pp + 1) * 1024],
                    scalar1=7, scalar2=None, op0=ALU.bitwise_and), sem=kb.sV)
            else:
                ph.emit("vector", lambda e, pp=pp, k=k: e.tensor_scalar(
                    out=nibN(k), in0=ib[:, pp * 1024:(pp + 1) * 1024],
                    scalar1=4, scalar2=15, op0=ALU.logical_shift_right,
                    op1=ALU.bitwise_and), sem=kb.sV)
                vm[k] = ph.emit("vector", lambda e, pp=pp, k=k: e.tensor_scalar(
                    out=nibM(k), in0=ib[:, pp * 1024:(pp + 1) * 1024],
                    scalar1=4, scalar2=7, op0=ALU.logical_shift_right,
                    op1=ALU.bitwise_and), sem=kb.sV)
            ph.wait("scalar", kb.sV, vm[k])
            ph.wait("scalar", kb.sP, pval.get(k - 3, 0))
            ph.wait("scalar", kb.sP, p_m0_hist.get(k - 2, 0))
            vs_[k] = ph.emit("scalar", lambda e, k=k: e.activation(
                out=tS(k), in_=nibN(k), func=AF.Sign,
                scale=-0.5, bias=kb.cc[:, CC_3p75:CC_3p75 + 1]), sem=kb.sA)
            ph.emit("scalar", lambda e, k=k: e.activation(
                out=tR1(k), in_=nibM(k), func=AF.Relu, scale=0.5,
                bias=kb.cc[:, CC_m2:CC_m2 + 1]), sem=kb.sA)
            vr2[k] = ph.emit("scalar", lambda e, k=k: e.activation(
                out=tR2(k), in_=nibM(k), func=AF.Relu,
                bias=kb.cc[:, CC_m6:CC_m6 + 1]), sem=kb.sA)

        def stageD2(k):
            if k < 0 or k >= NK:
                return
            ph.wait("gpsimd", kb.sA, vr2[k])
            ph.wait("gpsimd", kb.sV, pout.get(k - 2, 0))
            p_m0 = ph.emit("gpsimd", lambda e, k=k: e.tensor_tensor(
                out=tM0(k), in0=tR1(k), in1=tR2(k), op=ALU.add), sem=kb.sP)
            p_m0_hist[k] = p_m0
            ph.wait("vector", kb.sP, p_m0)
            ph.wait("vector", kb.sP, pval.get(k - 2, 0))
            vmag[k] = ph.emit("vector", lambda e, k=k: e.scalar_tensor_tensor(
                out=tMG(k), in0=nibM(k), scalar=0.5, in1=tM0(k),
                op0=ALU.mult, op1=ALU.add), sem=kb.sV)

        def stageG(k):
            if k < 0 or k >= NK:
                return
            pi, pl = k // 2, k % 2
            pp = pi % 2
            ph.wait("gpsimd", kb.sV, vmag[k])
            ph.wait("gpsimd", kb.sA, vs_[k])
            pval[k] = ph.emit("gpsimd", lambda e, k=k: e.tensor_tensor(
                out=tM0(k), in0=tMG(k), in1=tS(k), op=ALU.mult), sem=kb.sP)
            ph.wait("vector", kb.sP, pval[k])
            ph.wait("vector", kb.sd(f"sc{pp}"), ld_sc[pi])
            if pi >= 2:
                ph.wait("vector", kb.sM, m_tp.get(pi - 2, 0))
            pout[k] = ph.emit("vector", lambda e, pp=pp, pl=pl, k=k: e.tensor_tensor(
                out=vbf(pp).rearrange("p (b j two) -> p b j two", b=64, two=2)[:, :, :, pl],
                in0=tM0(k).rearrange("p (b j) -> p b j", b=64),
                in1=fb[:, F_SC + pp * 64: F_SC + (pp + 1) * 64].broadcast_to([128, 64, 16]),
                op=ALU.mult), sem=kb.sV)

        def finalize_piece(pi):
            if pi < 0 or pi >= NP:
                return
            pp = pi % 2
            dst = pieces[pi][2]
            for j in range(16):
                bk = len(tp_hist) % 2
                ph.wait("tensor", kb.sV, pout[2 * pi + 1])
                if len(tp_hist) >= 2:
                    ph.wait("tensor", kb.sV, tp_hist[-2])
                m = ph.emit("tensor", lambda e, pp=pp, j=j, bk=bk: e.transpose(
                    kb.psb[6 + bk][:, 0:128],
                    vbf(pp)[:, j * 128:(j + 1) * 128], kb.idb[:]), sem=kb.sM)
                ph.wait("vector", kb.sM, m)
                if j == 0 and pi >= 2:
                    ph.wait("vector", kb.sd(f"os{pp}"), st_p.get(pi - 2, 0))
                cp = ph.emit("vector", lambda e, pp=pp, j=j, bk=bk: e.tensor_copy(
                    out=ost(pp)[:, j * 128:(j + 1) * 128],
                    in_=kb.psb[6 + bk][:, 0:128]), sem=kb.sV)
                tp_hist.append(cp)
            m_tp[pi] = kb.sM.n
            ph.wait("scalar", kb.sV, tp_hist[-1])
            if isinstance(dst, tuple):
                st_p[pi] = ph.dma("scalar", dst[1].rearrange("n p m -> p n m"),
                                  ost(pp).rearrange("p (n m) -> p n m", n=16),
                                  kb.sd(f"os{pp}"))
            else:
                st_p[pi] = ph.dma("scalar", dst, ost(pp), kb.sd(f"os{pp}"))

        issue_piece_loads(0)
        issue_piece_loads(1)
        for k in range(NK + 3):
            if k < NK:
                stageDA(k)
            stageD2(k - 1)
            stageG(k - 2)
            if k >= 3 and k % 2 == 1:
                issue_piece_loads((k + 1) // 2)
            if k >= 3 and (k - 3) % 2 == 0:
                finalize_piece((k - 3) // 2)
            yield
        for pp in range(2):
            ph.wait("scalar", kb.sd(f"os{pp}"), kb.sd(f"os{pp}").n)

    def p6(self):
        kb = self
        ph = Ph(kb, "p6")
        T = kb.T
        NTP = T // 1024
        ar, fb, wb = kb.ar, kb.fb, kb.wb

        XN6, AO6, IST, IRH = 0, 16384, 32768, 34816

        def xn6(kc):
            return ar[:, XN6 + kc * 1024: XN6 + (kc + 1) * 1024]

        def ao6(kc):
            return ar[:, AO6 + kc * 1024: AO6 + (kc + 1) * 1024]

        def ist(s):
            return ar[:, IST + s * 1024: IST + (s + 1) * 1024]

        def irh(s):
            return ar[:, IRH + s * 2048: IRH + (s + 1) * 2048]

        def wgt(s):
            return wb[:, s * 2048:(s + 1) * 2048]

        def wut(s):
            return wb[:, 4096 + s * 2048: 4096 + (s + 1) * 2048]

        def wpt(s):
            return wb[:, 8192 + s * 512: 8192 + (s + 1) * 512]

        def wdt(s):
            return wb[:, 9216 + s * 1024: 9216 + (s + 1) * 1024]

        wg_free = [0, 0]
        wu_free = [0, 0]
        wp_free = [0, 0]
        wd_free = [0, 0]
        ir_free = [0, 0]
        ev_free = [0, 0]
        v_int_prev = {}
        ist_st = [0, 0]
        c_free = 0
        b_free = 0
        cb_free = 0
        x0_free = [0, 0]
        x1_free = {}

        for tp in range(NTP):
            for kc in range(DC):
                for half in range(2):
                    ph.dma("sync", xn6(kc)[:, half * 512:(half + 1) * 512],
                           kb.xn_s[tp * 2 + half, kc, :, :], kb.sd("xn6"))
            ld_xn6 = kb.sd("xn6").n
            ph.wait("tensor", kb.sd("xn6"), ld_xn6)
            ph.wait("tensor", kb.sV, c_free)
            for nc_ in range(128):
                gi = tp * 128 + nc_
                s = gi % 2
                r = nc_ % 2
                ph.wait("sync", kb.sM, wg_free[s])
                ld_wg = ph.dma("sync", wgt(s), kb.WgT_s[nc_, :, :], kb.sd(f"g6{s}"))
                ph.wait("sync", kb.sM, wu_free[s])
                ld_wu = ph.dma("sync", wut(s), kb.WuT_s[nc_, :, :], kb.sd(f"u6{s}"))
                ph.wait("tensor", kb.sd(f"g6{s}"), ld_wg)
                ph.wait("tensor", kb.sV, ev_free[r])
                for kc in range(DC):
                    for hf in range(2):
                        ph.emit("tensor", lambda e, kc=kc, hf=hf, s=s, r=r: e.matmul(
                            kb.ps[hf + 4 * r][:], wgt(s)[:, kc * 128:(kc + 1) * 128],
                            xn6(kc)[:, hf * 512:(hf + 1) * 512],
                            start=(kc == 0), stop=(kc == DC - 1)), sem=kb.sM)
                m_g = kb.sM.n
                wg_free[s] = m_g
                ph.wait("tensor", kb.sd(f"u6{s}"), ld_wu)
                for kc in range(DC):
                    for hf in range(2):
                        ph.emit("tensor", lambda e, kc=kc, hf=hf, s=s, r=r: e.matmul(
                            kb.ps[2 + hf + 4 * r][:], wut(s)[:, kc * 128:(kc + 1) * 128],
                            xn6(kc)[:, hf * 512:(hf + 1) * 512],
                            start=(kc == 0), stop=(kc == DC - 1)), sem=kb.sM)
                m_u = kb.sM.n
                wu_free[s] = m_u
                ssl = gi % 2
                v_int = 0
                for hf in range(2):
                    gb = kb.ps[hf + 4 * r]
                    ub = kb.ps[2 + hf + 4 * r]
                    A = fb[:, hf * 512:(hf + 1) * 512]
                    B = fb[:, 1024 + hf * 512: 1024 + (hf + 1) * 512]
                    ph.wait("scalar", kb.sM, m_g)
                    ph.wait("scalar", kb.sV, v_int_prev.get(hf, 0))
                    a_sq = ph.emit("scalar", lambda e, gb=gb, A=A: e.activation(
                        out=A, in_=gb[:], func=AF.Square), sem=kb.sA)
                    ph.wait("vector", kb.sA, a_sq)
                    ph.emit("vector", lambda e, A=A, B=B: e.tensor_scalar(
                        out=B, in0=A, scalar1=0.0356774081, scalar2=0.7978845608,
                        op0=ALU.mult, op1=ALU.add))
                    v_u = ph.emit("vector", lambda e, A=A, B=B, gb=gb: e.tensor_tensor(
                        out=A, in0=B, in1=gb[:], op=ALU.mult), sem=kb.sV)
                    ph.wait("scalar", kb.sV, v_u)
                    a_th = ph.emit("scalar", lambda e, A=A, B=B: e.activation(
                        out=B, in_=A, func=AF.Tanh), sem=kb.sA)
                    ph.wait("vector", kb.sA, a_th)
                    ph.wait("vector", kb.sM, m_u)
                    ph.emit("vector", lambda e, A=A, B=B, gb=gb: e.scalar_tensor_tensor(
                        out=A, in0=B, scalar=1.0, in1=gb[:],
                        op0=ALU.add, op1=ALU.mult))
                    if hf == 0:
                        ph.wait("vector", kb.sd(f"i6{ssl}"), ist_st[ssl])
                    v_int = ph.emit("vector", lambda e, A=A, ub=ub, ssl=ssl, hf=hf:
                                    e.scalar_tensor_tensor(
                        out=ist(ssl)[:, hf * 512:(hf + 1) * 512], in0=A, scalar=0.5,
                        in1=ub[:], op0=ALU.mult, op1=ALU.mult), sem=kb.sV)
                    v_int_prev[hf] = v_int
                ev_free[r] = v_int
                ph.wait("scalar", kb.sV, v_int)
                ist_st[ssl] = ph.dma("scalar", kb.int_s[nc_, :, tp * 1024:(tp + 1) * 1024],
                                     ist(ssl), kb.sd(f"i6{ssl}"))
            int_done = [kb.sd("i60").n, kb.sd("i61").n]
            b_free = kb.sV.n
            for kc in range(DC):
                ph.dma("sync", ao6(kc), kb.ao_s[kc, :, tp * 1024:(tp + 1) * 1024],
                       kb.sd("ao6"))
            ld_ao6 = kb.sd("ao6").n
            ph.wait("tensor", kb.sd("ao6"), ld_ao6)
            ph.wait("tensor", kb.sd("i60"), int_done[0])
            ph.wait("tensor", kb.sd("i61"), int_done[1])
            ph.wait("sync", kb.sd("i60"), int_done[0])
            ph.wait("sync", kb.sd("i61"), int_done[1])
            for og in range(4):
                for kc in range(DC):
                    s = kc % 2
                    ph.wait("gpsimd", kb.sM, wp_free[s])
                    ld_wp = ph.dma("gpsimd", wpt(s),
                                   kb.Wp[kc * 128:(kc + 1) * 128,
                                         og * 512:(og + 1) * 512], kb.sd(f"p6{s}"))
                    ph.wait("tensor", kb.sd(f"p6{s}"), ld_wp)
                    if kc == 0:
                        ph.wait("tensor", kb.sV, cb_free)
                    for j in range(4):
                        for hf in range(2):
                            ph.emit("tensor", lambda e, kc=kc, j=j, hf=hf, s=s: e.matmul(
                                kb.ps[j * 2 + hf][:], wpt(s)[:, j * 128:(j + 1) * 128],
                                ao6(kc)[:, hf * 512:(hf + 1) * 512],
                                start=(kc == 0), stop=False), sem=kb.sM)
                    wp_free[s] = kb.sM.n
                for np_ in range(64):
                    n0 = np_ * 2
                    s = np_ % 2
                    ph.wait("gpsimd", kb.sM, wd_free[s])
                    ld_wd = ph.dma("gpsimd", wdt(s).rearrange("p (n c) -> p n c", n=2),
                                   kb.WdT2[og, n0:n0 + 2, :, :].rearrange("n p c -> p n c"),
                                   kb.sd(f"d6{s}"))
                    ph.wait("sync", kb.sM, ir_free[s])
                    ld_ir = ph.dma("sync", irh(s).rearrange("p (n c) -> p n c", n=2),
                                   kb.int_s[n0:n0 + 2, :, tp * 1024:(tp + 1) * 1024]
                                   .rearrange("n p c -> p n c"),
                                   kb.sd(f"r6{s}"))
                    ph.wait("tensor", kb.sd(f"d6{s}"), ld_wd)
                    ph.wait("tensor", kb.sd(f"r6{s}"), ld_ir)
                    for dn in range(2):
                        n = n0 + dn
                        for j in range(4):
                            for hf in range(2):
                                ph.emit("tensor", lambda e, dn=dn, j=j, hf=hf, s=s,
                                        last=(n == 127): e.matmul(
                                    kb.ps[j * 2 + hf][:],
                                    wdt(s)[:, dn * 512 + j * 128: dn * 512 + (j + 1) * 128],
                                    irh(s)[:, dn * 1024 + hf * 512: dn * 1024 + (hf + 1) * 512],
                                    start=False, stop=last), sem=kb.sM)
                    wd_free[s] = ir_free[s] = kb.sM.n
                m_c = kb.sM.n
                for j in range(4):
                    oc = og * 4 + j
                    xsl = j % 2
                    ph.wait("sync", kb.sV, x0_free[xsl])
                    lx0a = ph.dma("sync", fb[:, 2048 + xsl * 1024: 2048 + xsl * 1024 + 512],
                                  kb.xf_s[tp * 2, oc, :, :], kb.sd(f"x6{xsl}"))
                    lx0b = ph.dma("sync", fb[:, 2048 + xsl * 1024 + 512: 2048 + (xsl + 1) * 1024],
                                  kb.xf_s[tp * 2 + 1, oc, :, :], kb.sd(f"x6{xsl}"))
                    for hf in range(2):
                        ph.wait("vector", kb.sM, m_c)
                        ph.wait("vector", kb.sd(f"x6{xsl}"), lx0b)
                        ph.wait("vector", kb.sd(f"y6{xsl}"), x1_free.get((xsl, hf), 0))
                        v_x1 = ph.emit("vector", lambda e, j=j, hf=hf, xsl=xsl: e.tensor_tensor(
                            out=fb[:, 4096 + xsl * 1024 + hf * 512: 4096 + xsl * 1024 + (hf + 1) * 512],
                            in0=kb.ps[j * 2 + hf][:],
                            in1=fb[:, 2048 + xsl * 1024 + hf * 512: 2048 + xsl * 1024 + (hf + 1) * 512],
                            op=ALU.add), sem=kb.sV)
                        ph.wait("scalar", kb.sV, v_x1)
                        x1_free[(xsl, hf)] = ph.dma(
                            "scalar", kb.xf_s[tp * 2 + hf, oc, :, :],
                            fb[:, 4096 + xsl * 1024 + hf * 512: 4096 + xsl * 1024 + (hf + 1) * 512],
                            kb.sd(f"y6{xsl}"))
                    x0_free[xsl] = kb.sV.n
                cb_free = kb.sV.n
            c_free = kb.sV.n
        for t_ in ("y60", "y61", "i60", "i61"):
            ph.wait("scalar", kb.sd(t_), kb.sd(t_).n)
        ph.run()


    def p7(self):
        kb = self
        self._p7_rows_free = 0
        ph = Ph(kb, "p7")
        ar32 = kb.ar32
        X7, SQ7, RB7, NB7 = 0, 8192, 9216, 9728
        XST = 20480

        def x7(kc):
            return ar32[:, X7 + kc * 512: X7 + (kc + 1) * 512]

        m_rows_prev = 0
        v_apply_prev = 0
        st_xn = [0, 0]
        a_sq_free = 0
        for tt in range(kb.TT):
            for kc in range(DC):
                if tt > 0 and kc == 0:
                    ph.wait("sync", kb.sV, v_apply_prev)
                ph.dma("sync", x7(kc), kb.xf_s[tt, kc, :, :], kb.sd("x7"))
            ld_x7 = kb.sd("x7").n
            a_sq = 0
            for kc in range(DC):
                sl = kc % 2
                ph.wait("scalar", kb.sd("x7"), ld_x7)
                if kc >= 2:
                    ph.wait("scalar", kb.sM, m_sq_mm[sl])
                elif tt > 0:
                    ph.wait("scalar", kb.sM, m_rows_prev)
                a_sq = ph.emit("scalar", lambda e, kc=kc, sl=sl: e.activation(
                    out=ar32[:, SQ7 + sl * 512: SQ7 + (sl + 1) * 512],
                    in_=x7(kc), func=AF.Square), sem=kb.sA)
                ph.wait("tensor", kb.sd("x7"), ld_x7)
                ph.wait("tensor", kb.sA, a_sq)
                if kc == 0:
                    ph.wait("tensor", kb.sV, self._p7_rows_free)
                ph.emit("tensor", lambda e, kc=kc: e.matmul(
                    kb.ps[0][0:1, 0:512], kb.onesc[:, 0:1], x7(kc),
                    start=(kc == 0), stop=(kc == DC - 1)), sem=kb.sM)
                mm_sq = ph.emit("tensor", lambda e, kc=kc, sl=sl: e.matmul(
                    kb.ps[1][0:1, 0:512], kb.onesc[:, 0:1],
                    ar32[:, SQ7 + sl * 512: SQ7 + (sl + 1) * 512],
                    start=(kc == 0), stop=(kc == DC - 1)), sem=kb.sM)
                if kc == 0:
                    m_sq_mm = [0, 0]
                m_sq_mm[sl] = mm_sq
            m_rows = kb.sM.n
            m_rows_prev = m_rows
            r = kb.row
            ph.wait("vector", kb.sM, m_rows)
            v1 = ph.emit("vector", lambda e: e.tensor_scalar(
                out=r[0:1, 512:1024], in0=kb.ps[0][0:1, 0:512],
                scalar1=1.0 / D, scalar2=None, op0=ALU.mult), sem=kb.sV)
            v2 = ph.emit("vector", lambda e: e.tensor_scalar(
                out=r[0:1, 1024:1536], in0=kb.ps[1][0:1, 0:512],
                scalar1=1.0 / D, scalar2=None, op0=ALU.mult), sem=kb.sV)
            ph.wait("vector", kb.sV, v2)
            v3 = ph.emit("vector", lambda e: e.tensor_tensor(
                out=r[0:1, 1536:2048], in0=r[0:1, 512:1024],
                in1=r[0:1, 512:1024], op=ALU.mult), sem=kb.sV)
            ph.wait("vector", kb.sV, v3)
            v4 = ph.emit("vector", lambda e: e.tensor_tensor(
                out=r[0:1, 1024:1536], in0=r[0:1, 1024:1536],
                in1=r[0:1, 1536:2048], op=ALU.subtract), sem=kb.sV)
            ph.wait("scalar", kb.sV, v4)
            a_sd = ph.emit("scalar", lambda e: e.activation(
                out=r[0:1, 1536:2048], in_=r[0:1, 1024:1536], func=AF.Sqrt,
                bias=kb.cc[0:1, CC_EPS_LN:CC_EPS_LN + 1]), sem=kb.sA)
            ph.wait("vector", kb.sA, a_sd)
            v5 = ph.emit("vector", lambda e: e.reciprocal(
                out=r[0:1, 1024:1536], in_=r[0:1, 1536:2048]), sem=kb.sV)
            ph.wait("vector", kb.sV, v5)
            v6 = ph.emit("vector", lambda e: e.scalar_tensor_tensor(
                out=r[0:1, 1536:2048], in0=r[0:1, 512:1024], scalar=-1.0,
                in1=r[0:1, 1024:1536], op0=ALU.mult, op1=ALU.mult), sem=kb.sV)
            ph.wait("tensor", kb.sV, v6)
            mb1 = ph.emit("tensor", lambda e: e.matmul(
                kb.ps[2][:, 0:512], kb.onesr[:, :], r[0:1, 1024:1536],
                start=True, stop=True), sem=kb.sM)
            mb2 = ph.emit("tensor", lambda e: e.matmul(
                kb.ps[3][:, 0:512], kb.onesr[:, :], r[0:1, 1536:2048],
                start=True, stop=True), sem=kb.sM)
            ph.wait("vector", kb.sM, mb2)
            cb1 = ph.emit("vector", lambda e: e.tensor_copy(
                out=ar32[:, RB7:RB7 + 512], in_=kb.ps[2][:, 0:512]), sem=kb.sV)
            cb2 = ph.emit("vector", lambda e: e.tensor_copy(
                out=ar32[:, NB7:NB7 + 512], in_=kb.ps[3][:, 0:512]), sem=kb.sV)
            self._p7_rows_free = cb2
            ph.wait("vector", kb.sV, cb2)
            for kc in range(DC):
                sl = kc % 2
                ph.wait("vector", kb.sd(f"xst{sl}"), st_xn[sl])
                ph.emit("vector", lambda e, kc=kc, sl=sl: e.tensor_tensor(
                    out=ar32[:, SQ7 + sl * 512: SQ7 + (sl + 1) * 512],
                    in0=x7(kc), in1=ar32[:, RB7:RB7 + 512], op=ALU.mult))
                v_ap = ph.emit("vector", lambda e, kc=kc, sl=sl: e.tensor_tensor(
                    out=kb.ar[:, XST + sl * 512: XST + (sl + 1) * 512],
                    in0=ar32[:, SQ7 + sl * 512: SQ7 + (sl + 1) * 512],
                    in1=ar32[:, NB7:NB7 + 512], op=ALU.add), sem=kb.sV)
                ph.wait("sync", kb.sV, v_ap)
                st_xn[sl] = ph.dma("sync", kb.xn_s[tt, kc, :, :],
                                   kb.ar[:, XST + sl * 512: XST + (sl + 1) * 512],
                                   kb.sd(f"xst{sl}"))
            v_apply_prev = kb.sV.n
        for sl in range(2):
            ph.wait("sync", kb.sd(f"xst{sl}"), kb.sd(f"xst{sl}").n)
        ph.run()

    def tern_prep(self, ph, w_in, dstT, n_rc, thr_ap=None, thr_col=None, kcols=2048,
                  out_scale=1.0, col0=0):
        kb = self
        fb, wb = kb.fb, kb.wb
        npiece = kcols // 2048 if kcols % 2048 == 0 else None
        for rc in range(n_rc):
            sl = rc % 2
            ph.wait("sync", kb.sV, self._w8vfree.get(sl, 0))
            ph.wait("sync", kb.sA, self._w8afree.get(sl, 0))
            lw = ph.dma("sync", fb[:, sl * 2048: sl * 2048 + kcols],
                        w_in[rc * 128:(rc + 1) * 128, col0:col0 + kcols],
                        kb.sd(f"w8{sl}"))
            ph.wait("vector", kb.sd(f"w8{sl}"), lw)
            ph.wait("vector", kb.sA, self._sg8_free.get(sl, 0))
            v_ab = ph.emit("vector", lambda e, sl=sl: e.scalar_tensor_tensor(
                out=fb[:, 4096:4096 + kcols], in0=fb[:, sl * 2048: sl * 2048 + kcols],
                scalar=-1.0, in1=fb[:, sl * 2048: sl * 2048 + kcols],
                op0=ALU.mult, op1=ALU.max), sem=kb.sV)
            if thr_ap is None:
                ph.wait("vector", kb.sV, v_ab)
                v_s1 = ph.emit("vector", lambda e: e.reduce_sum(
                    out=kb.smf[:, 0:16],
                    in_=fb[:, 4096:6144].rearrange("p (g d) -> p g d", g=16),
                    axis=AX.X), sem=kb.sV)
                ph.wait("vector", kb.sV, v_s1)
                v_s2 = ph.emit("vector", lambda e: e.tensor_scalar(
                    out=kb.smf[:, 0:16], in0=kb.smf[:, 0:16],
                    scalar1=1.0 / 128, scalar2=1e-8, op0=ALU.mult,
                    op1=ALU.max), sem=kb.sV)
                ph.wait("vector", kb.sV, v_s2)
                v_s3 = ph.emit("vector", lambda e: e.tensor_scalar(
                    out=kb.smf[:, 16:32], in0=kb.smf[:, 0:16],
                    scalar1=0.5, scalar2=None, op0=ALU.mult), sem=kb.sV)
                ph.wait("vector", kb.sV, v_s3)
                if out_scale != 1.0:
                    v_s3 = ph.emit("vector", lambda e, osc=out_scale: e.tensor_scalar(
                        out=kb.smf[:, 0:16], in0=kb.smf[:, 0:16],
                        scalar1=osc, scalar2=None, op0=ALU.mult), sem=kb.sV)
                    ph.wait("vector", kb.sV, v_s3)
                v_m = ph.emit("vector", lambda e: e.tensor_tensor(
                    out=wb[:, 0:2048].rearrange("p (g d) -> p g d", g=16),
                    in0=fb[:, 4096:6144].rearrange("p (g d) -> p g d", g=16),
                    in1=kb.smf[:, 16:32].broadcast_to([128, 16, 128]),
                    op=ALU.is_gt), sem=kb.sV)
            else:
                ph.wait("vector", kb.sV, v_ab)
                v_m = ph.emit("vector", lambda e: e.tensor_scalar(
                    out=wb[:, 0:kcols], in0=fb[:, 4096:4096 + kcols],
                    scalar1=thr_ap, scalar2=None, op0=ALU.is_ge), sem=kb.sV)
            ph.wait("scalar", kb.sd(f"w8{sl}"), lw)
            ph.wait("scalar", kb.sV, self._q8_free)
            a_sg = ph.emit("scalar", lambda e, sl=sl: e.activation(
                out=wb[:, 2048:2048 + kcols], in_=fb[:, sl * 2048: sl * 2048 + kcols],
                func=AF.Sign), sem=kb.sA)
            self._sg8_free[sl] = a_sg
            self._w8vfree[sl] = v_ab
            self._w8afree[sl] = a_sg
            ph.wait("vector", kb.sA, a_sg)
            ph.wait("vector", kb.sM, self._tp8_free)
            if thr_ap is None:
                v_q = ph.emit("vector", lambda e: e.tensor_tensor(
                    out=wb[:, 0:2048], in0=wb[:, 0:2048], in1=wb[:, 2048:4096],
                    op=ALU.mult), sem=kb.sV)
                ph.wait("vector", kb.sV, v_q)
                v_q = ph.emit("vector", lambda e: e.tensor_tensor(
                    out=wb[:, 4096:6144].rearrange("p (g d) -> p g d", g=16),
                    in0=wb[:, 0:2048].rearrange("p (g d) -> p g d", g=16),
                    in1=kb.smf[:, 0:16].broadcast_to([128, 16, 128]),
                    op=ALU.mult), sem=kb.sV)
            else:
                v_q = ph.emit("vector", lambda e: e.tensor_tensor(
                    out=wb[:, 4096:4096 + kcols], in0=wb[:, 0:kcols],
                    in1=wb[:, 2048:2048 + kcols], op=ALU.mult), sem=kb.sV)
            self._q8_free = v_q
            for j in range(kcols // 128):
                bk = self._tpbank % 4
                self._tpbank += 1
                ph.wait("tensor", kb.sV, v_q)
                if len(self._tph) >= 4:
                    ph.wait("tensor", kb.sV, self._tph[-4])
                m = ph.emit("tensor", lambda e, j=j, bk=bk: e.transpose(
                    kb.psb[4 + bk][:, 0:128],
                    wb[:, 4096 + j * 128: 4096 + (j + 1) * 128], kb.idb[:]), sem=kb.sM)
                ph.wait("vector", kb.sM, m)
                if j == 0:
                    ph.wait("vector", kb.sd(f"ts8{sl}"), self._st8.get(sl, 0))
                cp = ph.emit("vector", lambda e, j=j, bk=bk, sl=sl: e.tensor_copy(
                    out=wb[:, 8192 + sl * 2048 + j * 128: 8192 + sl * 2048 + (j + 1) * 128],
                    in_=kb.psb[4 + bk][:, 0:128]), sem=kb.sV)
                self._tph.append(cp)
            self._tp8_free = kb.sM.n
            ph.wait("sync", kb.sV, self._tph[-1])
            self._st8[sl] = ph.dma("sync", dstT[rc, :, col0:col0 + kcols],
                                   wb[:, 8192 + sl * 2048: 8192 + sl * 2048 + kcols],
                                   kb.sd(f"ts8{sl}"))

    def _p8_init(self):
        if not hasattr(self, "_w8vfree"):
            self._w8vfree = {}
            self._w8afree = {}
            self._sg8_free = {}
            self._q8_free = 0
            self._tp8_free = 0
            self._tph = []
            self._st8 = {}
            self._tpbank = 0

    def p8(self):
        kb = self
        self._p8_init()
        ph = Ph(kb, "p8a")
        self.tern_prep(ph, kb.wq_in, kb.wqT_s, 16, out_scale=128.0 ** -0.5)
        self.tern_prep(ph, kb.wk_in, kb.wkT_s, 16)
        self.tern_prep(ph, kb.wv_in, kb.wvT_s, 16)
        self.tern_prep(ph, kb.wo_in, kb.woT_s, 16)
        for sl in range(2):
            ph.wait("sync", kb.sd(f"ts8{sl}"), kb.sd(f"ts8{sl}").n)
        ph.run()
        self.p8b()
        self.p8c()

    def p8b(self):
        kb = self
        ph = Ph(kb, "p8b")
        ar, fb, wb = kb.ar, kb.fb, kb.wb
        T, TT = kb.T, kb.TT
        WT8, XN8 = 0, 32768
        mm_xn_prev = 0
        rp_users_prev = 0
        st_qk = {}
        ev_free = {}
        for pi, (src_s, dst_s) in enumerate([(kb.wqT_s, kb.qs_s), (kb.wkT_s, kb.ks_s)]):
            ph.wait("sync", kb.sM, mm_xn_prev)
            for oc in range(DC):
                ph.dma("sync", ar[:, WT8 + oc * 2048: WT8 + (oc + 1) * 2048],
                       src_s[oc, :, :], kb.sd("wt8"))
            ld_wt = kb.sd("wt8").n
            ph.wait("tensor", kb.sd("wt8"), ld_wt)
            for tt in range(TT):
                ph.wait("sync", kb.sM, mm_xn_prev)
                for kc in range(DC):
                    ph.dma("sync", ar[:, XN8 + kc * 512: XN8 + (kc + 1) * 512],
                           kb.xn_s[tt, kc, :, :], kb.sd("xn8"))
                ld_xn = kb.sd("xn8").n
                ph.wait("sync", kb.sV, rp_users_prev)
                l_cos = ph.dma("sync", fb[0:64, 0:512],
                               kb.c_rope[:, tt * 512:(tt + 1) * 512], kb.sd("rp8"))
                l_sin = ph.dma("sync", fb[0:64, 512:1024],
                               kb.c_rope[:, T + tt * 512: T + (tt + 1) * 512], kb.sd("rp8"))
                ph.wait("tensor", kb.sd("xn8"), ld_xn)
                for oc in range(DC):
                    b = oc % 4
                    ph.wait("tensor", kb.sV, ev_free.get(b, 0))
                    for kc in range(DC):
                        ph.emit("tensor", lambda e, oc=oc, kc=kc, b=b: e.matmul(
                            kb.ps[b][:],
                            ar[:, WT8 + oc * 2048 + kc * 128: WT8 + oc * 2048 + (kc + 1) * 128],
                            ar[:, XN8 + kc * 512: XN8 + (kc + 1) * 512],
                            start=(kc == 0), stop=(kc == DC - 1)), sem=kb.sM)
                    m_p = kb.sM.n
                    mm_xn_prev = m_p
                    sl = oc % 2
                    ph.wait("vector", kb.sM, m_p)
                    ph.wait("vector", kb.sd("rp8"), l_sin)
                    ph.wait("vector", kb.sd(f"qk8{sl}"), st_qk.get(sl, 0))
                    ph.emit("vector", lambda e, b=b: e.tensor_tensor(
                        out=fb[0:64, 1024:1536], in0=kb.ps[b][0:64, :],
                        in1=fb[0:64, 0:512], op=ALU.mult))
                    ph.emit("vector", lambda e, b=b: e.tensor_tensor(
                        out=fb[0:64, 1536:2048], in0=kb.ps[b][64:128, :],
                        in1=fb[0:64, 512:1024], op=ALU.mult))
                    ph.emit("vector", lambda e, sl=sl: e.tensor_tensor(
                        out=wb[:, 8192 + sl * 512: 8192 + sl * 512 + 512][0:64, :],
                        in0=fb[0:64, 1024:1536], in1=fb[0:64, 1536:2048],
                        op=ALU.subtract))
                    ph.emit("vector", lambda e, b=b: e.tensor_tensor(
                        out=fb[0:64, 1024:1536], in0=kb.ps[b][0:64, :],
                        in1=fb[0:64, 512:1024], op=ALU.mult))
                    ph.emit("vector", lambda e, b=b: e.tensor_tensor(
                        out=fb[0:64, 1536:2048], in0=kb.ps[b][64:128, :],
                        in1=fb[0:64, 0:512], op=ALU.mult))
                    v_hi = ph.emit("vector", lambda e, sl=sl: e.tensor_tensor(
                        out=wb[:, 8192 + sl * 512: 8192 + sl * 512 + 512][64:128, :],
                        in0=fb[0:64, 1024:1536], in1=fb[0:64, 1536:2048],
                        op=ALU.add), sem=kb.sV)
                    ev_free[b] = v_hi
                    ph.wait("sync", kb.sV, v_hi)
                    st_qk[sl] = ph.dma("sync", dst_s[oc, :, tt * 512:(tt + 1) * 512],
                                       wb[:, 8192 + sl * 512: 8192 + sl * 512 + 512],
                                       kb.sd(f"qk8{sl}"))
                rp_users_prev = kb.sV.n
        for sl in range(2):
            ph.wait("sync", kb.sd(f"qk8{sl}"), kb.sd(f"qk8{sl}").n)
        ph.run()

    def p8c(self):
        kb = self
        ph = Ph(kb, "p8c")
        ar, wb = kb.ar, kb.wb
        T, TT = kb.T, kb.TT
        WT8, XN8 = 0, 32768
        ph.wait("sync", kb.sM, kb.sM.n)
        for oc in range(DC):
            ph.dma("sync",
                   ar[:, WT8:WT8 + 32768].rearrange("p (kc o) -> p kc o", kc=16)
                   [:, :, oc * 128:(oc + 1) * 128],
                   kb.wvT_s[oc, :, :].rearrange("p (kc m) -> p kc m", kc=16),
                   kb.sd("wt8"))
        ld_wt = kb.sd("wt8").n
        ph.wait("tensor", kb.sd("wt8"), ld_wt)
        mm_xn_prev = 0
        ev_free = {}
        st_v = {}
        for tt in range(TT):
            ph.wait("sync", kb.sM, mm_xn_prev)
            for kc in range(DC):
                ph.dma("sync", ar[:, XN8 + kc * 512: XN8 + (kc + 1) * 512],
                       kb.xn_s[tt, kc, :, :], kb.sd("xn8"))
            ld_xn = kb.sd("xn8").n
            ph.wait("tensor", kb.sd("xn8"), ld_xn)
            for tc4 in range(4):
                tc = tt * 4 + tc4
                for dvg in range(4):
                    b = dvg
                    ph.wait("tensor", kb.sA, ev_free.get(b, 0))
                    for kc in range(DC):
                        ph.emit("tensor", lambda e, kc=kc, tc4=tc4, dvg=dvg, b=b: e.matmul(
                            kb.ps[b][:],
                            ar[:, XN8 + kc * 512 + tc4 * 128: XN8 + kc * 512 + (tc4 + 1) * 128],
                            ar[:, WT8 + kc * 2048 + dvg * 512: WT8 + kc * 2048 + (dvg + 1) * 512],
                            start=(kc == 0), stop=(kc == DC - 1)), sem=kb.sM)
                    m_v = kb.sM.n
                    mm_xn_prev = m_v
                    sl = dvg % 2
                    ph.wait("scalar", kb.sM, m_v)
                    ph.wait("scalar", kb.sd(f"v8{sl}"), st_v.get(sl, 0))
                    ev = ph.emit("scalar", lambda e, b=b, sl=sl: e.activation(
                        out=wb[:, 8192 + sl * 512: 8192 + (sl + 1) * 512],
                        in_=kb.ps[b][:], func=AF.Identity), sem=kb.sA)
                    ev_free[b] = ev
                    ph.wait("sync", kb.sA, ev)
                    st_v[sl] = ph.dma(
                        "sync",
                        kb.vs_s[dvg * 4:(dvg + 1) * 4, tc, :, :].rearrange("h p d -> p h d"),
                        wb[:, 8192 + sl * 512: 8192 + (sl + 1) * 512].rearrange(
                            "p (h d) -> p h d", h=4), kb.sd(f"v8{sl}"))
        for sl in range(2):
            ph.wait("sync", kb.sd(f"v8{sl}"), kb.sd(f"v8{sl}").n)
        ph.run()

    def p9(self):
        self._attn_init()
        self.attn("p9", causal=True, mask=True)


    def p10(self):
        kb = self
        self._p8_init()
        ph = Ph(kb, "p10a")
        fb = kb.fb
        specs = [(kb.w1_in, 44, 2048, [0], 0), (kb.w2_in, 44, 2048, [0], 1),
                 (kb.w3_in, 16, 2816, [0, 2816], 2)]
        g_free = [0, 0]
        for w_in, n_rc, wcols, col0s, idx in specs:
            parts = []
            pcount = 0
            for rc in range(n_rc):
                for c0 in col0s:
                    sl = pcount % 2
                    ph.wait("sync", kb.sV, g_free[sl])
                    lw = ph.dma("sync", fb[:, sl * 2848: sl * 2848 + wcols],
                                w_in[rc * 128:(rc + 1) * 128, c0:c0 + wcols],
                                kb.sd(f"gm{sl}"))
                    ph.wait("vector", kb.sd(f"gm{sl}"), lw)
                    col = 32 + (pcount % 44)
                    if pcount >= 44:
                        acc = True
                    if pcount < 44:
                        vv = ph.emit("vector", lambda e, sl=sl, col=col, wcols=wcols:
                                     e.reduce_sum(out=kb.smf[:, col:col + 1],
                                                  in_=fb[:, sl * 2848: sl * 2848 + wcols],
                                                  axis=AX.X, apply_absolute_value=True),
                                     sem=kb.sV)
                    else:
                        vv = ph.emit("vector", lambda e, sl=sl, wcols=wcols:
                                     e.reduce_sum(out=kb.smf[:, 80:81],
                                                  in_=fb[:, sl * 2848: sl * 2848 + wcols],
                                                  axis=AX.X, apply_absolute_value=True),
                                     sem=kb.sV)
                        ph.wait("vector", kb.sV, vv)
                        vv = ph.emit("vector", lambda e, col=col: e.tensor_tensor(
                            out=kb.smf[:, col:col + 1], in0=kb.smf[:, col:col + 1],
                            in1=kb.smf[:, 80:81], op=ALU.add), sem=kb.sV)
                    g_free[sl] = vv
                    pcount += 1
            nparts = min(pcount, 44)
            ph.wait("vector", kb.sV, kb.sV.n)
            v_t = ph.emit("vector", lambda e, nparts=nparts: e.reduce_sum(
                out=kb.smf[:, 81:82], in_=kb.smf[:, 32:32 + nparts], axis=AX.X),
                sem=kb.sV)
            ph.wait("tensor", kb.sV, v_t)
            m_t = ph.emit("tensor", lambda e: e.matmul(
                kb.ps[0][0:1, 0:1], kb.smf[:, 81:82], kb.onesc[:, 0:1],
                start=True, stop=True), sem=kb.sM)
            ph.wait("vector", kb.sM, m_t)
            nelem = n_rc * 128 * (wcols * len(col0s))
            v_g = ph.emit("vector", lambda e, nelem=nelem: e.tensor_scalar(
                out=kb.row[0:1, 0:1], in0=kb.ps[0][0:1, 0:1],
                scalar1=1.0 / nelem, scalar2=None, op0=ALU.mult), sem=kb.sV)
            ph.wait("vector", kb.sV, v_g)
            v_th = ph.emit("vector", lambda e: e.tensor_scalar(
                out=kb.row[0:1, 1:2], in0=kb.row[0:1, 0:1],
                scalar1=1e-8, scalar2=0.5, op0=ALU.add, op1=ALU.mult), sem=kb.sV)
            ph.wait("tensor", kb.sV, v_th)
            m_b = ph.emit("tensor", lambda e: e.matmul(
                kb.ps[1][:, 0:1], kb.onesr[:, :], kb.row[0:1, 1:2],
                start=True, stop=True), sem=kb.sM)
            ph.wait("vector", kb.sM, m_b)
            v_c = ph.emit("vector", lambda e, idx=idx: e.tensor_copy(
                out=kb.thr[:, idx:idx + 1], in_=kb.ps[1][:, 0:1]), sem=kb.sV)
            ph.wait("vector", kb.sV, v_c)
            v_z = ph.emit("vector", lambda e: e.memset(kb.smf[:, 32:76], 0.0), sem=kb.sV)
            g_free = [v_z, v_z]
        ph.run()
        ph2 = Ph(kb, "p10b")
        self.tern_prep(ph2, kb.w1_in, kb.w1T_s, 44, thr_ap=kb.thr[:, 0:1])
        self.tern_prep(ph2, kb.w2_in, kb.w2T_s, 44, thr_ap=kb.thr[:, 1:2])
        for c0, w in [(0, 2048), (2048, 2048), (4096, 1536)]:
            self.tern_prep(ph2, kb.w3_in, kb.w3T_s, 16, thr_ap=kb.thr[:, 2:3],
                           kcols=w, col0=c0)
        for sl in range(2):
            ph2.wait("sync", kb.sd(f"ts8{sl}"), kb.sd(f"ts8{sl}").n)
        ph2.run()
        self.p10c()

    def p10c(self):
        kb = self
        ph = Ph(kb, "p10c")
        ar, fb, wb = kb.ar, kb.fb, kb.wb
        T, TT = kb.T, kb.TT
        XN10, AO10, HID, W3SL, WOSL = 0, 8192, 16384, 38912, 50176
        wb32 = wb.bitcast(F32)

        def xn10(kc):
            return ar[:, XN10 + kc * 512: XN10 + (kc + 1) * 512]

        def ao10(kc):
            return ar[:, AO10 + kc * 512: AO10 + (kc + 1) * 512]

        def hid(hc):
            return ar[:, HID + hc * 512: HID + (hc + 1) * 512]

        w_free = [0, 0]
        wo_free = [0, 0]
        w3_free = [0, 0]
        sg_free = [0, 0]
        x1_bank_free = [0, 0]
        x2_bank_free = [0, 0]
        acc_bank_free = [0, 0]
        tp_bank_free = [0, 0]
        ost_free = [0] * 4
        x1sl_free = [0, 0]
        x2sl_free = [0, 0]
        mm_tile_prev = 0
        wl = 0

        for tt in range(TT):
            ph.wait("sync", kb.sM, mm_tile_prev)
            for kc in range(DC):
                ph.dma("sync", xn10(kc), kb.xn_s[tt, kc, :, :], kb.sd("xn0t"))
            ld_xn = kb.sd("xn0t").n
            for kc in range(DC):
                ph.dma("sync", ao10(kc), kb.ao_s[kc, :, tt * 512:(tt + 1) * 512],
                       kb.sd("ao0t"))
            ld_ao = kb.sd("ao0t").n
            ph.wait("tensor", kb.sd("xn0t"), ld_xn)
            ph.wait("tensor", kb.sd("ao0t"), ld_ao)
            for hc in range(44):
                bsl = hc % 2
                for which, wsrc in ((0, kb.w1T_s), (1, kb.w2T_s)):
                    sl = wl % 2
                    wl += 1
                    ph.wait("sync", kb.sM, w_free[sl])
                    lw = ph.dma("sync", wb[:, sl * 2048:(sl + 1) * 2048],
                                wsrc[hc, :, :], kb.sd(f"wa{sl}"))
                    ph.wait("tensor", kb.sd(f"wa{sl}"), lw)
                    bank = which * 2 + bsl
                    if which == 0:
                        ph.wait("tensor", kb.sA, x1_bank_free[bsl])
                    else:
                        ph.wait("tensor", kb.sV, x2_bank_free[bsl])
                    for kc in range(DC):
                        ph.emit("tensor", lambda e, kc=kc, sl=sl, bank=bank: e.matmul(
                            kb.ps[bank][:], wb[:, sl * 2048 + kc * 128: sl * 2048 + (kc + 1) * 128],
                            xn10(kc), start=(kc == 0), stop=(kc == DC - 1)), sem=kb.sM)
                    w_free[sl] = kb.sM.n
                    if which == 0:
                        m_x1 = kb.sM.n
                    else:
                        m_x2 = kb.sM.n
                ph.wait("scalar", kb.sM, m_x1)
                ph.wait("scalar", kb.sV, sg_free[bsl])
                a_sg = ph.emit("scalar", lambda e, bsl=bsl: e.activation(
                    out=fb[:, 1024 + bsl * 512: 1024 + (bsl + 1) * 512],
                    in_=kb.ps[bsl][:], func=AF.Sigmoid), sem=kb.sA)
                x1_bank_free[bsl] = a_sg
                ph.wait("vector", kb.sA, a_sg)
                ph.wait("vector", kb.sM, m_x2)
                v_h = ph.emit("vector", lambda e, hc=hc, bsl=bsl: e.tensor_tensor(
                    out=hid(hc), in0=fb[:, 1024 + bsl * 512: 1024 + (bsl + 1) * 512],
                    in1=kb.ps[2 + bsl][:], op=ALU.mult), sem=kb.sV)
                sg_free[bsl] = v_h
                x2_bank_free[bsl] = v_h
            v_hid_all = kb.sV.n
            ph.wait("tensor", kb.sV, v_hid_all)
            for oc in range(DC):
                osl = oc % 2
                ph.wait("sync", kb.sM, wo_free[osl])
                lwo = ph.dma("sync", ar[:, WOSL + osl * 2048: WOSL + (osl + 1) * 2048],
                             kb.woT_s[oc, :, :], kb.sd(f"wo{osl}"))
                ph.wait("sync", kb.sM, w3_free[osl])
                lw3 = ph.dma("sync", ar[:, W3SL + osl * 5632: W3SL + (osl + 1) * 5632],
                             kb.w3T_s[oc, :, :], kb.sd(f"w3{osl}"))
                bank = 4 + osl
                ph.wait("tensor", kb.sd(f"wo{osl}"), lwo)
                ph.wait("tensor", kb.sV, acc_bank_free[osl])
                for kc in range(DC):
                    ph.emit("tensor", lambda e, kc=kc, osl=osl, bank=bank: e.matmul(
                        kb.ps[bank][:],
                        ar[:, WOSL + osl * 2048 + kc * 128: WOSL + osl * 2048 + (kc + 1) * 128],
                        ao10(kc), start=(kc == 0), stop=False), sem=kb.sM)
                wo_free[osl] = kb.sM.n
                ph.wait("tensor", kb.sd(f"w3{osl}"), lw3)
                for hc in range(44):
                    ph.emit("tensor", lambda e, hc=hc, osl=osl, bank=bank: e.matmul(
                        kb.ps[bank][:],
                        ar[:, W3SL + osl * 5632 + hc * 128: W3SL + osl * 5632 + (hc + 1) * 128],
                        hid(hc), start=False, stop=(hc == 43)), sem=kb.sM)
                w3_free[osl] = kb.sM.n
                m_acc = kb.sM.n
                ph.wait("sync", kb.sV, x1sl_free[osl])
                lx1 = ph.dma("sync", fb[:, osl * 512:(osl + 1) * 512],
                             kb.xf_s[tt, oc, :, :], kb.sd(f"x10{osl}"))
                ph.wait("vector", kb.sM, m_acc)
                ph.wait("vector", kb.sd(f"x10{osl}"), lx1)
                ph.wait("vector", kb.sM, x2sl_free[osl])
                v_x2 = ph.emit("vector", lambda e, osl=osl, bank=bank: e.tensor_tensor(
                    out=fb[:, 2048 + osl * 512: 2048 + (osl + 1) * 512],
                    in0=kb.ps[bank][:], in1=fb[:, osl * 512:(osl + 1) * 512],
                    op=ALU.add), sem=kb.sV)
                x1sl_free[osl] = v_x2
                acc_bank_free[osl] = v_x2
                for tc in range(4):
                    tb = tc % 2
                    ph.wait("tensor", kb.sV, v_x2)
                    ph.wait("tensor", kb.sV, tp_bank_free[tb])
                    m_tp = ph.emit("tensor", lambda e, osl=osl, tc=tc, tb=tb: e.transpose(
                        kb.ps[6 + tb][:, 0:128],
                        fb[:, 2048 + osl * 512 + tc * 128: 2048 + osl * 512 + (tc + 1) * 128],
                        kb.idf[:]), sem=kb.sM)
                    ph.wait("vector", kb.sM, m_tp)
                    ph.wait("vector", kb.sd(f"yo{tc}"), ost_free[tc])
                    v_cp = ph.emit("vector", lambda e, tc=tc, tb=tb: e.tensor_copy(
                        out=wb32[:, 2048 + tc * 128: 2048 + (tc + 1) * 128],
                        in_=kb.ps[6 + tb][:, 0:128]), sem=kb.sV)
                    tp_bank_free[tb] = v_cp
                    ph.wait("scalar", kb.sV, v_cp)
                    ost_free[tc] = ph.dma(
                        "scalar",
                        kb.y_out[tt * 512 + tc * 128: tt * 512 + (tc + 1) * 128,
                                 oc * 128:(oc + 1) * 128],
                        wb32[:, 2048 + tc * 128: 2048 + (tc + 1) * 128],
                        kb.sd(f"yo{tc}"))
                x2sl_free[osl] = kb.sM.n
            mm_tile_prev = kb.sM.n
        for tc in range(4):
            ph.wait("scalar", kb.sd(f"yo{tc}"), kb.sd(f"yo{tc}").n)
        ph.run()



def host_consts(T):
    idb = np.eye(128).astype(BF)
    idf = np.eye(128, dtype=np.float32)
    onesc = np.ones((128, 2), np.float32)
    onesb = np.ones((128, 2)).astype(BF)
    onesr = np.ones((1, 128), np.float32)
    cc = np.zeros((128, NCC), np.float32)
    cc[:, CC_EPS_LN] = 1e-5
    cc[:, CC_3p75] = 3.75
    cc[:, CC_m8] = -8.0
    cc[:, CC_m10] = -10.0
    cc[:, CC_m4] = -4.0
    cc[:, CC_m6] = -6.0
    cc[:, CC_m2] = -2.0
    half = 64
    inv_freq = 1.0 / 10000.0 ** (np.arange(half, dtype=np.float32) / half)
    freqs = np.outer(inv_freq, np.arange(T, dtype=np.float32))
    rope = np.concatenate([np.cos(freqs), np.sin(freqs)], axis=1).astype(np.float32)
    p = np.arange(128)[:, None]
    f = np.arange(512)[None, :]
    cmask = np.stack([(f >= p + j * 128) for j in range(4)]).astype(BF)
    return dict(idb=idb, idf=idf, onesc=onesc, onesb=onesb, onesr=onesr,
                cc=cc, rope=rope, cmask=cmask)


def core_inputs(inputs, core, T):
    m = dict(
        x=np.ascontiguousarray(inputs["L0_attn_x"][core, :T]),
        W_attn=inputs["L0_attn_W_attn"],
        W_proj=inputs["L0_attn_W_proj"],
        lamvec=np.concatenate([
            inputs["L0_attn_lambda_q1"], inputs["L0_attn_lambda_k1"],
            inputs["L0_attn_lambda_q2"], inputs["L0_attn_lambda_k2"]]).reshape(1, -1),
        g_pk=inputs["L0_ffn_gate_W_packed"], g_sc=inputs["L0_ffn_gate_scales"],
        u_pk=inputs["L0_ffn_up_W_packed"], u_sc=inputs["L0_ffn_up_scales"],
        d_pk=inputs["L0_ffn_down_W_packed"], d_sc=inputs["L0_ffn_down_scales"],
        wq=inputs["L1_attn_w_q"], wk=inputs["L1_attn_w_k"],
        wv=inputs["L1_attn_w_v"], wo=inputs["L1_attn_w_o"],
        w1=inputs["L1_ffn_w1"], w2=inputs["L1_ffn_w2"], w3=inputs["L1_ffn_w3"],
    )
    m.update(host_consts(T))
    m = {k: np.ascontiguousarray(v) for k, v in m.items()}
    return m


def build(T, dump=(), phases=("p0", "p1")):
    nc = bass.Bass("TRN2", target_bir_lowering=False)
    kb = KB(nc, T, dump=dump)
    kb.declare()
    for p in phases:
        getattr(kb, p)()
    return nc, kb


ALL_PHASES = ("p0", "p1", "p2", "p35", "p6", "p7", "p8", "p9", "p10")


def kernel(**inputs):
    T = 4096
    nc, kb = build(T, phases=ALL_PHASES)
    in_maps = [core_inputs(inputs, c, T) for c in range(8)]
    res = run_bass_kernel_spmd(nc, in_maps, core_ids=list(range(8)))
    kb.ctx.close()
    return np.stack([res.results[c]["y"] for c in range(8)], axis=0)
